# version 03
# baseline (speedup 1.0000x reference)
import numpy as np
import ml_dtypes
from contextlib import ExitStack

import concourse.bass as bass
import concourse.mybir as mybir
from concourse.bass_utils import run_bass_kernel_spmd

F32 = mybir.dt.float32
BF16 = mybir.dt.bfloat16
I32 = mybir.dt.int32
AF = mybir.ActivationFunctionType
ALU = mybir.AluOpType
AX = mybir.AxisListType
BF = ml_dtypes.bfloat16

NCORES = 8
B, S, D = 4, 4096, 1024
TOK = B * S
TC = TOK // NCORES
EPS = 1e-5

ENGINE_SEM = {"tensor": "ts", "vector": "vs", "scalar": "ss", "gpsimd": "gs"}


def _np2dt(d):
    d = np.dtype(d)
    if d == np.float32:
        return F32
    if d == np.int32:
        return I32
    if d == np.dtype(ml_dtypes.bfloat16):
        return BF16
    raise ValueError(d)


class Sem:
    def __init__(self, h):
        self.h = h
        self.n = 0


class Prog:
    ENGINES = ("tensor", "vector", "scalar", "gpsimd", "sync")

    def __init__(self, name="prog"):
        self.name = name
        self.nc = bass.Bass("TRN2", target_bir_lowering=False)
        self.st = ExitStack()
        self.streams = {e: [] for e in self.ENGINES}
        self._sems = {}
        self._uniq = 0
        for s in ("ts", "vs", "ss", "gs", "ld", "cl", "os", "bar"):
            self.sem(s)
        self._psum = []
        self._psum_mark = []
        self._psum_rr = 0

    def sem(self, name):
        if name not in self._sems:
            self._sems[name] = Sem(self.st.enter_context(self.nc.semaphore(name)))
        return self._sems[name]

    def sbuf(self, name, shape, dtype):
        if getattr(self, "_scope", None) is not None:
            name = self._prefix + name
            return self._scope.enter_context(self.nc.sbuf_tensor(name, list(shape), dtype))[:]
        return self.st.enter_context(self.nc.sbuf_tensor(name, list(shape), dtype))[:]

    def push_scope(self, prefix):
        self._scope = ExitStack()
        self._prefix = prefix
        return self._scope

    def pop_scope(self):
        self._scope.close()
        self._scope = None

    def psum_t(self, name, shape, dtype=F32):
        if not hasattr(self, "_psum_named"):
            self._psum_named = {}
        if name in self._psum_named:
            return self._psum_named[name]
        ap = self.st.enter_context(self.nc.psum_tensor(name, list(shape), dtype))[:]
        self._psum_named[name] = ap
        return ap

    def dram_in(self, name, shape, dtype=F32):
        return self.nc.dram_tensor(name, list(shape), dtype, kind="ExternalInput").ap()

    def dram_out(self, name, shape, dtype=F32):
        return self.nc.dram_tensor(name, list(shape), dtype, kind="ExternalOutput").ap()

    def dram_tmp(self, name, shape, dtype=F32):
        return self.nc.dram_tensor(name, list(shape), dtype, kind="Internal").ap()

    def const(self, arr, name=None):
        self._uniq += 1
        return self.nc.inline_tensor(np.ascontiguousarray(arr),
                                     name or f"const{self._uniq}").ap()

    def regconsts(self, *values):
        vals = [float(v) for v in values]
        arr = np.tile(np.array([vals], dtype=np.float32), (128, 1))
        ap, dep = self.const_sbuf(arr, name=f"rc{self._uniq}")
        for i, v in enumerate(vals):
            self.nc.const_aps.register(F32, v, ap[:, i:i + 1])
        return dep

    def const_sbuf(self, arr, name=None, q="sync"):
        self._uniq += 1
        nm = name or f"csb{self._uniq}"
        dr = self.const(arr, nm + "_d")
        ap = self.sbuf(nm, list(np.asarray(arr).shape), _np2dt(np.asarray(arr).dtype))
        dep = self.dma(q, ap, dr)
        return ap, dep

    def on(self, eng, fn):
        self.streams[eng].append(fn)

    def wait(self, eng, dep):
        deps = dep if isinstance(dep, list) else [dep]
        for sm, v in deps:
            if v > 0:
                self.on(eng, lambda e, _s=sm.h, _v=v: e.wait_ge(_s, _v))

    def run(self, eng, fn, mark=False):
        if mark:
            sm = self.sem(ENGINE_SEM[eng])
            def f(e, _fn=fn, _s=sm.h):
                _fn(e).then_inc(_s, 1)
            self.on(eng, f)
            sm.n += 1
            return (sm, sm.n)
        self.on(eng, lambda e, _fn=fn: _fn(e))
        return None

    def dma(self, q, dst, src, sem_name=None):
        sm = self.sem(sem_name or ("cl" if q == "gpsimd" else "ld"))
        def f(e, _d=dst, _s=src, _sm=sm.h, _v=sm.n + 16):
            e.dma_start(out=_d, in_=_s).then_inc(_sm, 16)
            e.wait_ge(_sm, _v)
        self.on(q, f)
        sm.n += 16
        return (sm, sm.n)

    def barrier(self, engines=None):
        engines = list(engines or self.ENGINES)
        bs = self.sem("bar")
        target = bs.n + len(engines)
        for eng in engines:
            self.on(eng, lambda e, _s=bs.h: e.sem_inc(_s, 1))
        bs.n = target
        for eng in engines:
            self.on(eng, lambda e, _s=bs.h, _v=target: e.wait_ge(_s, _v))

    def psum_pool(self, n=8, cols=512):
        if self._psum:
            return
        for i in range(n):
            self._psum.append(self.psum_t(f"pp{i}", [128, cols], F32))
            self._psum_mark.append(None)

    def psum_acquire(self):
        i = self._psum_rr % len(self._psum)
        self._psum_rr += 1
        w = self._psum_mark[i]
        if w is not None:
            self.wait("tensor", w)
        return i, self._psum[i]

    def psum_release(self, i, dep):
        self._psum_mark[i] = dep

    def build(self):
        with self.nc.Block() as block:
            for e in self.ENGINES:
                ops = self.streams[e]
                def body(eng, _ops=ops):
                    for f in _ops:
                        f(eng)
                getattr(block, e)(body)
        return self.nc


class ExtIO:
    def __init__(self, p):
        self.p = p

    def inp(self, name, shape, dtype=F32):
        return self.p.dram_in(name, shape, dtype)

    def out(self, name, shape, dtype=F32):
        return self.p.dram_out(name, shape, dtype)


class FusedIO:
    def __init__(self, p, stage, registry, act_inputs, final_out=None):
        self.p = p
        self.stage = stage
        self.reg = registry
        self.act = act_inputs
        self.final = final_out or {}

    def inp(self, name, shape, dtype=F32):
        if name in self.act:
            return self.reg[self.act[name]]
        return self.p.dram_in(self.stage + "." + name, shape, dtype)

    def out(self, name, shape, dtype=F32):
        key = self.stage + "." + name
        if key in self.final:
            ap = self.p.dram_out(self.final[key], shape, dtype)
        else:
            ap = self.p.dram_tmp(key, shape, dtype)
        self.reg[key] = ap
        return ap


def emit_matmul_group(p, ps, tiles, mark=True):
    n = len(tiles)
    dep = None
    for k, (lt, rt, mr, ncol) in enumerate(tiles):
        def mm(e, _ps=ps, _lt=lt, _rt=rt, _st=(k == 0), _sp=(k == n - 1),
               _m=mr, _n=ncol):
            return e.matmul(_ps[0:_m, 0:_n], _lt, _rt, start=_st, stop=_sp)
        dep = p.run("tensor", mm, mark=(mark and k == n - 1))
    return dep


class Linear:
    pass


def emit_linear(p, *, nk, nm, chunks, wf, xf, evac, extra_tiles=None):
    for ch, ncol in chunks:
        for m in range(nm):
            i, ps = p.psum_acquire()
            tiles = []
            for k in range(nk):
                lt = wf(k, m)
                tiles.append((lt, xf(k, ch), lt.shape[-1], ncol))
            if extra_tiles is not None:
                tiles += extra_tiles(m, ch)
            dep = emit_matmul_group(p, ps, tiles)
            cdep = evac(m, ch, ps, dep)
            p.psum_release(i, cdep)


def emit_ln(p, ctx, *, xbf, nk, ncol, ones_bf, ones_f, sq2, stat, out_cb,
            eps=EPS, affine=None, xdep=None):
    if xdep is not None:
        p.wait("scalar", xdep)
        p.wait("tensor", xdep)
    s1r = stat[:, 0:ncol]
    varr = stat[:, ncol:2 * ncol]
    nmr = stat[:, 2 * ncol:3 * ncol]
    rsr = stat[:, 3 * ncol:4 * ncol]
    i1, ps1 = p.psum_acquire()
    i2, ps2 = p.psum_acquire()
    sqdep = [ctx.get("sqdep0"), ctx.get("sqdep1")]
    for k in range(nk):
        x = xbf(k)
        def mm1(e, _ps=ps1, _o=ones_bf, _x=x, _st=(k == 0), _sp=(k == nk - 1), _n=ncol):
            return e.matmul(_ps[0:1, 0:_n], _o, _x, start=_st, stop=_sp)
        p.run("tensor", mm1)
        if sqdep[k % 2] is not None:
            p.wait("scalar", sqdep[k % 2])
        sdep = p.run("scalar", lambda e, _o=sq2[k % 2][:, 0:ncol], _x=x:
                     e.activation(_o, _x, AF.Square), mark=True)
        p.wait("tensor", sdep)
        def mm2(e, _ps=ps2, _o=ones_bf, _x=sq2[k % 2][:, 0:ncol], _st=(k == 0),
                _sp=(k == nk - 1), _n=ncol):
            return e.matmul(_ps[0:1, 0:_n], _o, _x, start=_st, stop=_sp)
        mdep = p.run("tensor", mm2, mark=True)
        sqdep[k % 2] = mdep
    ctx["sqdep0"], ctx["sqdep1"] = sqdep
    s12dep = mdep
    p.wait("vector", s12dep)
    p.run("vector", lambda e: e.tensor_scalar(nmr, ps1[0:1, 0:ncol], -1.0 / D, None, ALU.mult))
    vdep = p.run("vector", lambda e: e.tensor_scalar(varr, ps2[0:1, 0:ncol], 1.0 / D, None, ALU.mult), mark=True)
    p.wait("scalar", vdep)
    sdep = p.run("scalar", lambda e: e.activation(s1r, nmr, AF.Square), mark=True)
    p.wait("vector", sdep)
    vdep = p.run("vector", lambda e: e.tensor_sub(varr, varr, s1r), mark=True)
    p.wait("scalar", vdep)
    p.run("scalar", lambda e: e.activation(varr, varr, AF.Ln, bias=eps))
    sdep = p.run("scalar", lambda e: e.activation(rsr, varr, AF.Exp, scale=-0.5), mark=True)
    p.psum_release(i1, sdep)
    p.psum_release(i2, vdep)
    ib1, bnm = p.psum_acquire()
    ib2, brs = p.psum_acquire()
    p.wait("tensor", sdep)
    p.run("tensor", lambda e: e.matmul(bnm[:, 0:ncol], ones_f, nmr, start=True, stop=True))
    bdep = p.run("tensor", lambda e: e.matmul(brs[:, 0:ncol], ones_f, rsr, start=True, stop=True), mark=True)
    last = None
    p.wait("vector", bdep)
    for k in range(nk):
        last = out_cb(k, bnm[:, 0:ncol], brs[:, 0:ncol]) or last
    p.psum_release(ib1, last)
    p.psum_release(ib2, last)
    return last


def col_tiles(v):
    v = np.asarray(v)
    return np.ascontiguousarray(v.reshape(-1, 128).T)


def _f32(a):
    return np.ascontiguousarray(np.asarray(a, dtype=np.float32))


def evac_act(p, out_ap, ps_ap, func, mmdep, bias=0.0, scale=1.0, eng="scalar"):
    p.wait(eng, mmdep)
    if eng == "scalar":
        dep = p.run(eng, lambda e: e.activation(out_ap, ps_ap, func, bias=bias, scale=scale), mark=True)
    else:
        raise NotImplementedError
    return dep


def chunk_barrier(p):
    os_ = p.sem("os")
    p.wait("sync", (os_, os_.n))
    p.barrier()


def build_l0pre(p=None, io=None):
    p = p or Prog("l0pre")
    io = io or ExtIO(p)
    NH = 3
    x_ov = io.inp("x_ov", [D, TC + NH])
    halorow = io.inp("halorow", [1, NH])
    w_in = io.inp("w_in", [D, D])
    ln_g = io.inp("ln_g", [128, 8])
    ln_b = io.inp("ln_b", [128, 8])
    b_in = io.inp("b_in", [128, 8])
    convw = io.inp("convw", [128, 8, 4])
    convb = io.inp("convb", [128, 8])
    w_dt = io.inp("w_dt", [D, D])
    b_dt = io.inp("b_dt", [128, 8])
    w_bc = io.inp("w_bc", [D, 32])
    w1T = io.inp("w1T", [4096, D])
    w2 = io.inp("w2", [4096, D])
    b1c = io.inp("b1c", [128, 32])
    b2r = io.inp("b2r", [1, D])

    XN = io.out("XN", [D, TC])
    GATE = io.out("GATE", [D, TC])
    LEFTC = io.out("LEFTC", [D, TC])
    DT = io.out("DT", [D, TC])
    DTLEFT = io.out("DTLEFT", [D, TC])
    BCo = io.out("BC", [32, TC])
    WCOMB = io.out("WCOMB", [D, D])
    CE = io.out("CE", [1, D])

    p.regconsts(EPS, 1.0, -1.0, -0.5)
    ones_bf, ob_dep = p.const_sbuf(np.ones((128, 1), dtype=BF))
    ones_f, of_dep = p.const_sbuf(np.ones((1, 128), dtype=np.float32))
    p.wait("tensor", [ob_dep, of_dep])

    p.psum_pool(8)

    stW = ExitStack()
    sbW = lambda name, shape, dtype: stW.enter_context(p.nc.sbuf_tensor(name, list(shape), dtype))[:]
    wc_sb = sbW("wc_sb", [128, 8 * 1024], F32)
    ce_sb = sbW("ce_sb", [1, 1024], F32)
    b2_sb = sbW("b2_sb", [1, 1024], F32)
    w1b = sbW("w1b", [128, 16 * 1024], BF16)
    w2b = sbW("w2b", [128, 16 * 1024], BF16)
    b1b = sbW("b1b", [128, 16], BF16)
    d_b2 = p.dma("sync", b2_sb, b2r)
    for half in range(2):
        deps = []
        for kk in range(16):
            krow = half * 16 + kk
            deps.append(p.dma("gpsimd", w1b[:, kk * 1024:(kk + 1) * 1024],
                              w1T[krow * 128:(krow + 1) * 128, :]))
            deps.append(p.dma("gpsimd", w2b[:, kk * 1024:(kk + 1) * 1024],
                              w2[krow * 128:(krow + 1) * 128, :]))
        deps.append(p.dma("gpsimd", b1b, b1c[:, half * 16:(half + 1) * 16]))
        p.wait("tensor", deps[-1])
        p.wait("tensor", deps[-3])
        for m in range(8):
            for n in range(2):
                i, ps = p.psum_acquire()
                tiles = [(w1b[:, kk * 1024 + m * 128: kk * 1024 + (m + 1) * 128],
                          w2b[:, kk * 1024 + n * 512: kk * 1024 + (n + 1) * 512],
                          128, 512) for kk in range(16)]
                dep = emit_matmul_group(p, ps, tiles)
                p.wait("vector", dep)
                col = m * 1024 + n * 512
                if half == 0:
                    cdep = p.run("vector", lambda e, _c=col, _ps=ps:
                                 e.tensor_copy(wc_sb[:, _c:_c + 512], _ps), mark=True)
                else:
                    cdep = p.run("vector", lambda e, _c=col, _ps=ps:
                                 e.tensor_add(wc_sb[:, _c:_c + 512], wc_sb[:, _c:_c + 512], _ps), mark=True)
                p.psum_release(i, cdep)
        for n in range(2):
            i, ps = p.psum_acquire()
            tiles = [(b1b[:, kk:kk + 1], w2b[:, kk * 1024 + n * 512: kk * 1024 + (n + 1) * 512],
                      1, 512) for kk in range(16)]
            dep = emit_matmul_group(p, ps, tiles)
            p.wait("vector", dep)
            if half == 0:
                cdep = p.run("vector", lambda e, _n=n, _ps=ps:
                             e.tensor_copy(ce_sb[:, _n * 512:(_n + 1) * 512], _ps[0:1, :]), mark=True)
            else:
                cdep = p.run("vector", lambda e, _n=n, _ps=ps:
                             e.tensor_add(ce_sb[:, _n * 512:(_n + 1) * 512],
                                          ce_sb[:, _n * 512:(_n + 1) * 512], _ps[0:1, :]), mark=True)
            p.psum_release(i, cdep)
        chunk_barrier(p)
    p.wait("vector", d_b2)
    vdep = p.run("vector", lambda e: e.tensor_add(ce_sb, ce_sb, b2_sb), mark=True)
    p.wait("sync", vdep)
    p.dma("sync", WCOMB.rearrange("(m p) t -> p m t", p=128),
          wc_sb.rearrange("p (m t) -> p m t", m=8), "os")
    p.dma("sync", CE, ce_sb, "os")
    chunk_barrier(p)
    stW.close()

    w_in_sb = p.sbuf("w_in_sb", [128, 8 * 1024], BF16)
    w_dt_sb = p.sbuf("w_dt_sb", [128, 8 * 1024], BF16)
    w_bc_sb = p.sbuf("w_bc_sb", [128, 8 * 32], BF16)
    lng_sb = p.sbuf("lng_sb", [128, 8], F32)
    lnb_sb = p.sbuf("lnb_sb", [128, 8], F32)
    bin_sb = p.sbuf("bin_sb", [128, 8], F32)
    cw_sb = p.sbuf("cw_sb", [128, 8 * 4], F32)
    cb_sb = p.sbuf("cb_sb", [128, 8], F32)
    bdt_sb = p.sbuf("bdt_sb", [128, 8], F32)
    hm_sb = p.sbuf("hm_sb", [1, NH], F32)
    wdeps = []
    for k in range(8):
        wdeps.append(p.dma("gpsimd", w_in_sb[:, k * 1024:(k + 1) * 1024], w_in[k * 128:(k + 1) * 128, :]))
        wdeps.append(p.dma("gpsimd", w_dt_sb[:, k * 1024:(k + 1) * 1024], w_dt[k * 128:(k + 1) * 128, :]))
        wdeps.append(p.dma("gpsimd", w_bc_sb[:, k * 32:(k + 1) * 32], w_bc[k * 128:(k + 1) * 128, :]))
    for dst, src in ((lng_sb, ln_g), (lnb_sb, ln_b), (bin_sb, b_in),
                     (cw_sb, convw.rearrange("p m k -> p (m k)")), (cb_sb, convb),
                     (bdt_sb, b_dt), (hm_sb, halorow)):
        wdeps.append(p.dma("sync", dst, src))

    xf = p.sbuf("xf", [128, 8, 256], F32)
    xb = p.sbuf("xb", [128, 8, 256], BF16)
    xn_f = p.sbuf("xn_f", [128, 8, 256], F32)
    xn2f = p.sbuf("xn2f", [128, 8, 256], F32)
    xn2b = p.sbuf("xn2b", [128, 8, 256], BF16)
    gate_f = p.sbuf("gate_f", [128, 8, 256], F32)
    left_sb = p.sbuf("left_sb", [128, 8, TC + NH], F32)
    lc_f = p.sbuf("lc_f", [128, 8, 256], F32)
    lc_b = p.sbuf("lc_b", [128, 8, 256], BF16)
    dt_f = p.sbuf("dt_f", [128, 8, 256], F32)
    dtl_f = p.sbuf("dtl_f", [128, 8, 256], F32)
    bc_sb = p.sbuf("bc_sb", [32, TC], F32)
    sq2 = [p.sbuf("sq0", [128, 512], BF16), p.sbuf("sq1", [128, 512], BF16)]
    stat = p.sbuf("stat", [1, 2048], F32)
    cacc = [p.sbuf("cacc0", [128, 512], F32), p.sbuf("cacc1", [128, 512], F32)]
    spt = p.sbuf("spt", [128, 512], F32)

    x_r = x_ov.rearrange("(m p) t -> p m t", p=128)
    XN_r = XN.rearrange("(m p) t -> p m t", p=128)
    GATE_r = GATE.rearrange("(m p) t -> p m t", p=128)
    LEFTC_r = LEFTC.rearrange("(m p) t -> p m t", p=128)
    DT_r = DT.rearrange("(m p) t -> p m t", p=128)
    DTLEFT_r = DTLEFT.rearrange("(m p) t -> p m t", p=128)

    p.wait("tensor", wdeps[-1])
    p.wait("scalar", wdeps[-1])
    p.wait("vector", wdeps[-1])
    chunks = [(-1, NH)] + [(ch, 256) for ch in range(8)]
    for ch, ncol in chunks:
        c0 = 0 if ch < 0 else NH + ch * 256
        o0 = None if ch < 0 else ch * 256
        ldep = p.dma("sync", xf[:, :, 0:ncol], x_r[:, :, c0:c0 + ncol])
        cdep = p.dma("gpsimd", xb[:, :, 0:ncol], x_r[:, :, c0:c0 + ncol])
        p.wait("tensor", cdep)
        p.wait("scalar", cdep)
        p.wait("vector", ldep)

        def out_cb(k, bnm, brs, _nc=ncol, _store=o0 is not None):
            p.run("vector", lambda e, _k=k: e.tensor_add(xn_f[:, _k, 0:_nc], xf[:, _k, 0:_nc], bnm))
            p.run("vector", lambda e, _k=k: e.tensor_mul(xn_f[:, _k, 0:_nc], xn_f[:, _k, 0:_nc], brs))
            p.run("vector", lambda e, _k=k: e.tensor_scalar(
                xn2f[:, _k, 0:_nc], xn_f[:, _k, 0:_nc], lng_sb[:, _k:_k + 1], lnb_sb[:, _k:_k + 1], ALU.mult, ALU.add))
            vdep = p.run("vector", lambda e, _k=k: e.tensor_copy(xn2b[:, _k, 0:_nc], xn2f[:, _k, 0:_nc]), mark=True)
            p.wait("scalar", vdep)
            sdep = p.run("scalar", lambda e, _k=k: e.activation(gate_f[:, _k, 0:_nc], xn2f[:, _k, 0:_nc], AF.Silu), mark=True)
            return vdep
        lndep = emit_ln(p, {}, xbf=lambda k: xb[:, k, 0:ncol], nk=8, ncol=ncol,
                        ones_bf=ones_bf, ones_f=ones_f, sq2=sq2, stat=stat, out_cb=out_cb)
        if o0 is not None:
            sS = p.sem("ss")
            p.wait("sync", (sS, sS.n))
            p.wait("sync", lndep)
            p.dma("sync", XN_r[:, :, o0:o0 + ncol], xn_f[:, :, 0:ncol], "os")
            p.dma("sync", GATE_r[:, :, o0:o0 + ncol], gate_f[:, :, 0:ncol], "os")

        p.wait("tensor", lndep)
        evacLd = []
        def evacL(m, chx, ps, dep, _nc=ncol, _c0=c0):
            d = evac_act(p, left_sb[:, m, _c0:_c0 + _nc], ps[:, 0:_nc], AF.Identity,
                         dep, bias=bin_sb[:, m:m + 1])
            evacLd.append(d)
            return d
        emit_linear(p, nk=8, nm=8, chunks=[(0, ncol)],
                    wf=lambda k, m: w_in_sb[:, k * 1024 + m * 128: k * 1024 + (m + 1) * 128],
                    xf=lambda k, _c: xn2b[:, k, 0:ncol], evac=evacL)
        if ch < 0:
            ih, psh = p.psum_acquire()
            hdep = p.run("tensor", lambda e: e.matmul(psh[:, 0:NH], ones_f, hm_sb, start=True, stop=True), mark=True)
            p.wait("vector", hdep)
            p.wait("vector", evacLd[-1])
            mdep = None
            for m in range(8):
                mdep = p.run("vector", lambda e, _m=m: e.tensor_mul(
                    left_sb[:, _m, 0:NH], left_sb[:, _m, 0:NH], psh[:, 0:NH]), mark=True)
            p.psum_release(ih, mdep)
        else:
            p.wait("vector", evacLd[-1])
            for m in range(8):
                a = cacc[m % 2]
                def c_first(e, _m=m, _nc=ncol, _c=c0):
                    return e.tensor_scalar(a[:, 0:_nc], left_sb[:, _m, _c - 3:_c - 3 + _nc],
                                           cw_sb[:, _m * 4:_m * 4 + 1], None, ALU.mult)
                p.run("vector", c_first)
                for tap in range(1, 4):
                    def c_tap(e, _m=m, _t=tap, _nc=ncol, _c=c0):
                        return e.scalar_tensor_tensor(
                            a[:, 0:_nc], left_sb[:, _m, _c - 3 + _t:_c - 3 + _t + _nc],
                            cw_sb[:, _m * 4 + _t:_m * 4 + _t + 1], a[:, 0:_nc],
                            ALU.mult, ALU.add)
                    vdep = p.run("vector", c_tap, mark=(tap == 3))
                p.wait("scalar", vdep)
                sdep = p.run("scalar", lambda e, _m=m, _nc=ncol: e.activation(
                    lc_f[:, _m, 0:_nc], a[:, 0:_nc], AF.Silu, bias=cb_sb[:, _m:_m + 1]), mark=True)
                p.wait("vector", sdep)
                p.run("vector", lambda e, _m=m, _nc=ncol: e.tensor_copy(
                    lc_b[:, _m, 0:_nc], lc_f[:, _m, 0:_nc]))
            vdep = p.run("vector", lambda e: e.engine_nop(), mark=True)
            sS = p.sem("ss")
            p.wait("sync", vdep)
            p.dma("sync", LEFTC_r[:, :, o0:o0 + ncol], lc_f[:, :, 0:ncol], "os")

            p.wait("tensor", vdep)
            dstate = []
            def evacD(m, chx, ps, dep, _nc=ncol):
                p.wait("scalar", dep)
                if dstate:
                    p.wait("scalar", dstate[-1])
                p.run("scalar", lambda e, _m=m: e.activation(spt[:, 0:_nc], ps[:, 0:_nc], AF.Abs, bias=bdt_sb[:, _m:_m + 1]))
                p.run("scalar", lambda e: e.activation(spt[:, 0:_nc], spt[:, 0:_nc], AF.Exp, scale=-1.0))
                p.run("scalar", lambda e: e.activation(spt[:, 0:_nc], spt[:, 0:_nc], AF.Ln, bias=1.0))
                sdep = p.run("scalar", lambda e, _m=m: e.activation(dt_f[:, _m, 0:_nc], ps[:, 0:_nc], AF.Relu, bias=bdt_sb[:, _m:_m + 1]), mark=True)
                p.wait("vector", sdep)
                vdep = p.run("vector", lambda e, _m=m: e.tensor_add(dt_f[:, _m, 0:_nc], dt_f[:, _m, 0:_nc], spt[:, 0:_nc]), mark=True)
                dstate.append(vdep)
                p.run("vector", lambda e, _m=m: e.tensor_mul(dtl_f[:, _m, 0:_nc], dt_f[:, _m, 0:_nc], lc_f[:, _m, 0:_nc]))
                return vdep
            emit_linear(p, nk=8, nm=8, chunks=[(0, ncol)],
                        wf=lambda k, m: w_dt_sb[:, k * 1024 + m * 128: k * 1024 + (m + 1) * 128],
                        xf=lambda k, _c: lc_b[:, k, 0:ncol], evac=evacD)
            vS = p.sem("vs")
            p.wait("sync", (vS, vS.n))
            p.dma("sync", DT_r[:, :, o0:o0 + ncol], dt_f[:, :, 0:ncol], "os")
            p.dma("sync", DTLEFT_r[:, :, o0:o0 + ncol], dtl_f[:, :, 0:ncol], "os")

            def evacB(m, chx, ps, dep, _nc=ncol, _o0=o0):
                p.wait("vector", dep)
                return p.run("vector", lambda e: e.tensor_copy(bc_sb[:, _o0:_o0 + _nc], ps[0:32, 0:_nc]), mark=True)
            emit_linear(p, nk=8, nm=1, chunks=[(0, ncol)],
                        wf=lambda k, m: w_bc_sb[:, k * 32:(k + 1) * 32],
                        xf=lambda k, _c: lc_b[:, k, 0:ncol], evac=evacB)
        chunk_barrier(p)
    vS = p.sem("vs")
    p.wait("sync", (vS, vS.n))
    p.dma("sync", BCo, bc_sb, "os")
    chunk_barrier(p)
    return p


def prep_l0pre(ins, c):
    b, half = c // 2, c % 2
    t0 = half * TC
    x = np.asarray(ins["L0_attn_x"])
    xs = x[b, t0 - 3:t0 + TC].T if t0 >= 3 else None
    if t0 == 0:
        xs = np.concatenate([np.zeros((3, D), np.float32), x[b, 0:TC]], axis=0).T
        halo = np.zeros((1, 3), np.float32)
    else:
        xs = x[b, t0 - 3:t0 + TC].T
        halo = np.ones((1, 3), np.float32)
    e = c
    d = {
        "x_ov": _f32(xs), "halorow": halo,
        "w_in": _f32(ins["L0_attn_w_in"]),
        "ln_g": col_tiles(ins["L0_attn_ln_g"]), "ln_b": col_tiles(ins["L0_attn_ln_b"]),
        "b_in": col_tiles(ins["L0_attn_b_in"]),
        "convw": _f32(np.asarray(ins["L0_attn_conv_w"])[:, 0, :].reshape(8, 128, 4).transpose(1, 0, 2)),
        "convb": col_tiles(ins["L0_attn_conv_b"]),
        "w_dt": _f32(ins["L0_attn_w_dt"]), "b_dt": col_tiles(ins["L0_attn_b_dt"]),
        "w_bc": _f32(ins["L0_attn_w_bc"]),
        "w1T": _f32(np.asarray(ins["L3_ffn_w1"])[e].T),
        "w2": _f32(np.asarray(ins["L3_ffn_w2"])[e]),
        "b1c": col_tiles(np.asarray(ins["L3_ffn_b1"])[e]),
        "b2r": _f32(np.asarray(ins["L3_ffn_b2"])[e][None, :]),
    }
    return d


def build_l0scan(p=None, io=None):
    p = p or Prog("l0scan")
    io = io or ExtIO(p)
    TG, TH = 4096, 2048
    dt_s = io.inp("dt_s", [512, TG])
    dtl_s = io.inp("dtl_s", [512, TG])
    alog = io.inp("alog", [128, 64])
    Brow = io.inp("Brow", [16, TG])
    Crow = io.inp("Crow", [16, TG])
    Y = io.out("Y", [512, TG])
    CAR = io.out("CAR", [128, 64])

    p.psum_pool(8)
    al_sb = p.sbuf("al_sb", [128, 64], F32)
    acol = p.sbuf("acol", [128, 64], F32)
    dt_sb = p.sbuf("dt_sb", [128, 4, TH], F32)
    dtl_sb = p.sbuf("dtl_sb", [128, 4, TH], F32)
    y_sb = p.sbuf("y_sb", [128, 4, TH], F32)
    bbc = p.sbuf("bbc", [128, TH], F32)
    cbc = p.sbuf("cbc", [128, TH], F32)
    dA = [p.sbuf("dA0", [128, TH], F32), p.sbuf("dA1", [128, TH], F32)]
    dBx = p.sbuf("dBx", [128, TH], F32)
    hbuf = p.sbuf("hbuf", [128, TH], F32)
    carry = p.sbuf("carry", [128, 64], F32)

    d1 = p.dma("sync", al_sb, alog)
    p.wait("scalar", d1)
    sdep = p.run("scalar", lambda e: e.activation(acol, al_sb, AF.Exp), mark=True)
    p.wait("vector", sdep)
    p.run("vector", lambda e: e.tensor_scalar_mul(acol, acol, -1.0))
    p.barrier()

    dt_r = dt_s.rearrange("(m p) t -> p m t", p=128)
    dtl_r = dtl_s.rearrange("(m p) t -> p m t", p=128)
    Y_r = Y.rearrange("(m p) t -> p m t", p=128)
    for tch in range(2):
        c0 = tch * TH
        p.dma("sync", dt_sb, dt_r[:, :, c0:c0 + TH])
        ldd = p.dma("sync", dtl_sb, dtl_r[:, :, c0:c0 + TH])
        p.wait("scalar", ldd)
        p.wait("vector", ldd)
        first_bc = True
        for n in range(16):
            bd1 = p.dma("sync", bbc, Brow[n:n + 1, c0:c0 + TH].broadcast_to([128, TH]))
            bd2 = p.dma("sync", cbc, Crow[n:n + 1, c0:c0 + TH].broadcast_to([128, TH]))
            p.wait("vector", bd2)
            for dtile in range(4):
                ci = dtile * 16 + n
                pp = (n * 4 + dtile) % 2
                sdep = p.run("scalar", lambda e, _d=dtile, _c=ci, _p=pp: e.activation(
                    dA[_p], dt_sb[:, _d, :], AF.Exp, scale=acol[:, _c:_c + 1]), mark=True)
                p.run("vector", lambda e, _d=dtile: e.tensor_mul(dBx, dtl_sb[:, _d, :], bbc))
                p.wait("vector", sdep)
                init = 0.0 if tch == 0 else carry[:, ci:ci + 1]
                vdep = p.run("vector", lambda e, _p=pp, _i=init: e.tensor_tensor_scan(
                    hbuf, dA[_p], dBx, _i, ALU.mult, ALU.add), mark=True)
                p.run("vector", lambda e: e.tensor_mul(dBx, hbuf, cbc))
                if tch == 0:
                    p.run("vector", lambda e, _c=ci: e.tensor_copy(carry[:, _c:_c + 1], hbuf[:, TH - 1:TH]))
                if n == 0:
                    vdep = p.run("vector", lambda e, _d=dtile: e.tensor_copy(y_sb[:, _d, :], dBx), mark=True)
                else:
                    vdep = p.run("vector", lambda e, _d=dtile: e.tensor_add(y_sb[:, _d, :], y_sb[:, _d, :], dBx), mark=True)
                p.wait("scalar", vdep)
            p.wait("sync", vdep)
        p.dma("sync", Y_r[:, :, c0:c0 + TH], y_sb, "os")
        if tch == 0:
            p.dma("sync", CAR, carry, "os")
        chunk_barrier(p)
    return p


def prep_l0scan(ins, outs1, c):
    b, half = c // 2, c % 2
    d0 = half * 512
    dt_b = np.concatenate([outs1[2 * b]["DT"], outs1[2 * b + 1]["DT"]], axis=1)
    dtl_b = np.concatenate([outs1[2 * b]["DTLEFT"], outs1[2 * b + 1]["DTLEFT"]], axis=1)
    bc_b = np.concatenate([outs1[2 * b]["BC"], outs1[2 * b + 1]["BC"]], axis=1)
    alog = np.asarray(ins["L0_attn_A_log"], np.float32)[d0:d0 + 512]
    al = np.concatenate([alog[i * 128:(i + 1) * 128, :] for i in range(4)], axis=1)
    return {
        "dt_s": _f32(dt_b[d0:d0 + 512]), "dtl_s": _f32(dtl_b[d0:d0 + 512]),
        "alog": _f32(al),
        "Brow": _f32(bc_b[0:16]),
        "Crow": _f32(bc_b[16:32]),
    }


def build_l0post(p=None, io=None):
    p = p or Prog("l0post")
    io = io or ExtIO(p)
    x_sh = io.inp("x_sh", [D, TC])
    XNi = io.inp("XNi", [D, TC])
    LEFTCi = io.inp("LEFTCi", [D, TC])
    GATEi = io.inp("GATEi", [D, TC])
    Yre = io.inp("Yre", [D, TC])
    dcol = io.inp("dcol", [128, 8])
    w_out = io.inp("w_out", [D, D])
    b_out = io.inp("b_out", [128, 8])
    qT = io.inp("qT", [6, D, D], I32)
    scT = io.inp("scT", [6, D, D])
    qb = io.inp("qb", [6, 128, 8])
    qT_bias_r = qb.rearrange("i p e -> p i e")
    laT = io.inp("laT", [6, D, 32])
    lbT = io.inp("lbT", [6, 32, D])
    lng = io.inp("lng", [3, 128, 8])
    lnb = io.inp("lnb", [3, 128, 8])
    wq_dT = io.inp("wq_dT", [D, 512])
    bq_d = io.inp("bq_d", [128, 4])
    wkv_dT = io.inp("wkv_dT", [D, 64])
    bkv_d = io.inp("bkv_d", [64, 1])

    X1 = io.out("X1", [D, TC])
    XN1 = io.out("XN1", [D, TC], BF16)
    CQ = io.out("CQ", [512, TC], BF16)
    CKV = io.out("CKV", [64, TC], BF16)

    x1p_d = p.dram_tmp("x1p_d", [D, TC])
    ybuf_d = p.dram_tmp("ybuf_d", [D, TC], BF16)

    p.regconsts(EPS, 1.0, -1.0, -0.5, 1.0 / 7.5)
    ones_bf, ob_dep = p.const_sbuf(np.ones((128, 1), dtype=BF))
    ones_f, of_dep = p.const_sbuf(np.ones((1, 128), dtype=np.float32))
    p.wait("tensor", [ob_dep, of_dep])
    p.psum_pool(7)
    psu_ded = p.psum_t("psu_ded", [128, 512], F32)

    r = lambda ap: ap.rearrange("(m p) t -> p m t", p=128)

    st1 = ExitStack()
    sb1 = lambda nm, shp, dt_: st1.enter_context(p.nc.sbuf_tensor(nm, list(shp), dt_))[:]
    w_out_sb = sb1("w_out_sb", [128, 8 * 1024], BF16)
    dcol_sb = sb1("dcol_sb", [128, 8], F32)
    bout_sb = sb1("bout_sb", [128, 8], F32)
    yre_c = sb1("yre_c", [128, 8, 512], F32)
    lc_c = sb1("lc_c", [128, 8, 512], F32)
    gt_c = sb1("gt_c", [128, 8, 512], F32)
    xn_c = sb1("xn_c", [128, 8, 512], F32)
    x_c = sb1("x_c", [128, 8, 512], F32)
    zf = sb1("zf", [128, 8, 512], F32)
    zb = sb1("zb", [128, 8, 512], BF16)
    x1p = sb1("x1p", [128, 8, 512], F32)
    for k in range(8):
        p.dma("gpsimd", w_out_sb[:, k * 1024:(k + 1) * 1024], w_out[k * 128:(k + 1) * 128, :])
    p.dma("sync", dcol_sb, dcol)
    wdep = p.dma("sync", bout_sb, b_out)
    p.wait("tensor", wdep)
    p.wait("vector", wdep)
    p.wait("scalar", wdep)
    for ch in range(4):
        c0 = ch * 512
        p.dma("sync", yre_c, r(Yre)[:, :, c0:c0 + 512])
        p.dma("sync", lc_c, r(LEFTCi)[:, :, c0:c0 + 512])
        p.dma("sync", gt_c, r(GATEi)[:, :, c0:c0 + 512])
        p.dma("sync", xn_c, r(XNi)[:, :, c0:c0 + 512])
        ld = p.dma("sync", x_c, r(x_sh)[:, :, c0:c0 + 512])
        p.wait("vector", ld)
        for m in range(8):
            p.run("vector", lambda e, _m=m: e.scalar_tensor_tensor(
                zf[:, _m, :], lc_c[:, _m, :], dcol_sb[:, _m:_m + 1], yre_c[:, _m, :], ALU.mult, ALU.add))
            p.run("vector", lambda e, _m=m: e.tensor_mul(zf[:, _m, :], zf[:, _m, :], gt_c[:, _m, :]))
            vdep = p.run("vector", lambda e, _m=m: e.tensor_copy(zb[:, _m, :], zf[:, _m, :]), mark=True)
        p.wait("tensor", vdep)

        def evacP(m, chx, ps, dep):
            p.wait("scalar", dep)
            sdep = p.run("scalar", lambda e, _m=m: e.activation(
                x1p[:, _m, :], ps, AF.Identity, bias=bout_sb[:, _m:_m + 1]), mark=True)
            p.wait("vector", sdep)
            p.run("vector", lambda e, _m=m: e.tensor_add(x1p[:, _m, :], x1p[:, _m, :], xn_c[:, _m, :]))
            return p.run("vector", lambda e, _m=m: e.tensor_add(x1p[:, _m, :], x1p[:, _m, :], x_c[:, _m, :]), mark=True)
        emit_linear(p, nk=8, nm=8, chunks=[(0, 512)],
                    wf=lambda k, m: w_out_sb[:, k * 1024 + m * 128:k * 1024 + (m + 1) * 128],
                    xf=lambda k, _c: zb[:, k, :], evac=evacP)
        vS = p.sem("vs")
        p.wait("sync", (vS, vS.n))
        p.wait("gpsimd", (vS, vS.n))
        p.dma("gpsimd", r(x1p_d)[:, :, c0:c0 + 512], x1p, "os")
        chunk_barrier(p)
    st1.close()

    st2 = ExitStack()
    sb2 = lambda nm, shp, dt_: st2.enter_context(p.nc.sbuf_tensor(nm, list(shp), dt_))[:]
    h_sb = sb2("h_sb", [128, 8, TC], F32)
    hbf = sb2("hbf", [128, 8, 512], BF16)
    q1f = sb2("q1f", [128, 8, 512], F32)
    q1b = sb2("q1b", [128, 8, 512], BF16)
    ybf = sb2("ybf", [128, 8, TC], BF16)
    Wd = sb2("Wd", [128, 8 * 1024], BF16)
    qraw = sb2("qraw", [128, 8 * 1024], BF16)
    scb = sb2("scb", [128, 8 * 1024], BF16)
    la_sb = sb2("la_sb", [128, 8 * 32], BF16)
    lb_sb = sb2("lb_sb", [32, 1024], BF16)
    ub = sb2("ub", [32, 512], BF16)
    qb_all = sb2("qb_all", [128, 6, 8], F32)
    lng_all = sb2("lng_all", [128, 3, 8], F32)
    lnb_all = sb2("lnb_all", [128, 3, 8], F32)
    t1s = [sb2("t1a", [128, 512], F32), sb2("t1b", [128, 512], F32)]
    sq2 = [sb2("sq2a", [128, 512], BF16), sb2("sq2b", [128, 512], BF16)]
    stat = sb2("stat", [1, 2048], F32)

    p.dma("gpsimd", qb_all, qT_bias_r)
    p.dma("gpsimd", lng_all, lng.rearrange("i p e -> p i e"))
    bdep_all = p.dma("gpsimd", lnb_all, lnb.rearrange("i p e -> p i e"))
    p.wait("scalar", bdep_all)
    p.wait("vector", bdep_all)
    hdep = p.dma("sync", h_sb, r(XNi))
    p.wait("vector", hdep)

    def load_dequant(i):
        dq = []
        for k in range(8):
            dq.append(p.dma("gpsimd", qraw[:, k * 1024:(k + 1) * 1024], qT[i, k * 128:(k + 1) * 128, :]))
            dq.append(p.dma("gpsimd", scb[:, k * 1024:(k + 1) * 1024], scT[i, k * 128:(k + 1) * 128, :]))
            dq.append(p.dma("gpsimd", la_sb[:, k * 32:(k + 1) * 32], laT[i, k * 128:(k + 1) * 128, :]))
        dq.append(p.dma("gpsimd", lb_sb, lbT[i]))
        p.wait("scalar", dq[-1])
        sdep = p.run("scalar", lambda e: e.activation(qraw, qraw, AF.Identity, scale=1.0 / 7.5, bias=-1.0), mark=True)
        p.wait("vector", sdep)
        vdep = p.run("vector", lambda e: e.tensor_mul(Wd, qraw, scb), mark=True)
        p.wait("tensor", vdep)
        chunk_barrier(p)
        return vdep

    for rnd in range(3):
        i0, i1 = 2 * rnd, 2 * rnd + 1
        qb_sb = qb_all[:, i0, :]
        gam_sb = lng_all[:, rnd, :]
        bet_sb = lnb_all[:, rnd, :]
        gdep = bdep_all
        load_dequant(i0)
        lnctx = {}
        ywdeps = []
        for ch in range(4):
            c0 = ch * 512
            tS = p.sem("ts")
            p.wait("vector", (tS, tS.n))
            if ywdeps:
                p.wait("scalar", ywdeps[-1])
            vdep = p.run("vector", lambda e, _c=c0: e.tensor_copy(hbf, h_sb[:, :, _c:_c + 512]), mark=True)
            p.wait("tensor", vdep)
            psu = psu_ded
            du = emit_matmul_group(p, psu, [(la_sb[:, k * 32:(k + 1) * 32], hbf[:, k, :], 32, 512) for k in range(8)])
            p.wait("vector", du)
            udep = p.run("vector", lambda e: e.tensor_copy(ub, psu[0:32, :]), mark=True)
            p.wait("tensor", udep)

            q1deps = []
            def evacQ1(m, chx, ps, dep, _c0=c0, _qb=qb_sb):
                p.wait("scalar", dep)
                sdep = p.run("scalar", lambda e, _m=m: e.activation(
                    q1f[:, _m, :], ps, AF.Identity, bias=_qb[:, _m:_m + 1]), mark=True)
                p.wait("vector", sdep)
                d = p.run("vector", lambda e, _m=m: e.tensor_copy(q1b[:, _m, :], q1f[:, _m, :]), mark=True)
                q1deps.append(d)
                return d
            emit_linear(p, nk=8, nm=8, chunks=[(0, 512)],
                        wf=lambda k, m: Wd[:, k * 1024 + m * 128:k * 1024 + (m + 1) * 128],
                        xf=lambda k, _c: hbf[:, k, :], evac=evacQ1,
                        extra_tiles=lambda m, _c: [(lb_sb[:, m * 128:(m + 1) * 128], ub, 128, 512)])
            p.wait("tensor", q1deps[-1])
            if ch == 0:
                p.wait("vector", gdep)

            def out_cb(k, bnm, brs, _c0=c0, _g=gam_sb, _b=bet_sb):
                t1 = t1s[k % 2]
                p.run("vector", lambda e, _k=k: e.tensor_add(t1, q1f[:, _k, :], bnm))
                p.run("vector", lambda e: e.tensor_mul(t1, t1, brs))
                vdep = p.run("vector", lambda e, _k=k: e.tensor_scalar(
                    t1, t1, _g[:, _k:_k + 1], _b[:, _k:_k + 1], ALU.mult, ALU.add), mark=True)
                p.wait("scalar", vdep)
                sdep = p.run("scalar", lambda e, _k=k: e.activation(ybf[:, _k, _c0:_c0 + 512], t1, AF.Gelu), mark=True)
                p.wait("vector", sdep)
                return sdep
            lnd = emit_ln(p, lnctx, xbf=lambda k: q1b[:, k, :], nk=8, ncol=512,
                    ones_bf=ones_bf, ones_f=ones_f, sq2=sq2, stat=stat, out_cb=out_cb,
                    xdep=q1deps[-1])
            ywdeps.append(lnd)
            chunk_barrier(p)
        oS = p.sem("os")
        p.wait("sync", (oS, oS.n))
        p.barrier()
        load_dequant(i1)
        qb_sb = qb_all[:, i1, :]
        for ch in range(4):
            c0 = ch * 512
            psu2 = psu_ded
            du2 = emit_matmul_group(p, psu2, [(la_sb[:, k * 32:(k + 1) * 32], ybf[:, k, c0:c0 + 512], 32, 512) for k in range(8)])
            p.wait("vector", du2)
            udep2 = p.run("vector", lambda e: e.tensor_copy(ub, psu2[0:32, :]), mark=True)
            p.wait("tensor", udep2)

            def evacQ2(m, chx, ps, dep, _c0=c0, _qb=qb_sb):
                p.wait("vector", dep)
                return p.run("vector", lambda e, _m=m: e.scalar_tensor_tensor(
                    h_sb[:, _m, _c0:_c0 + 512], ps, _qb[:, _m:_m + 1],
                    h_sb[:, _m, _c0:_c0 + 512], ALU.add, ALU.add), mark=True)
            emit_linear(p, nk=8, nm=8, chunks=[(0, 512)],
                        wf=lambda k, m: Wd[:, k * 1024 + m * 128:k * 1024 + (m + 1) * 128],
                        xf=lambda k, _c, _c0=c0: ybf[:, k, _c0:_c0 + 512], evac=evacQ2,
                        extra_tiles=lambda m, _c: [(lb_sb[:, m * 128:(m + 1) * 128], ub, 128, 512)])
            chunk_barrier(p)
        chunk_barrier(p)

    wq_sb = sb2("wq_sb", [128, 8 * 512], BF16)
    wkv_sb = sb2("wkv_sb", [128, 8 * 64], BF16)
    bq_sb = sb2("bq_sb", [128, 4], F32)
    bkv_sb = sb2("bkv_sb", [64, 1], F32)
    x1c = q1f
    x1b = q1b
    xn1b = ybf[:, :, 0:512]
    cqb = sb2("cqb", [128, 4, 512], BF16)
    for k in range(8):
        p.dma("gpsimd", wq_sb[:, k * 512:(k + 1) * 512], wq_dT[k * 128:(k + 1) * 128, :])
        p.dma("gpsimd", wkv_sb[:, k * 64:(k + 1) * 64], wkv_dT[k * 128:(k + 1) * 128, :])
    p.dma("sync", bq_sb, bq_d)
    wdep = p.dma("sync", bkv_sb, bkv_d)
    p.wait("tensor", wdep)
    p.wait("scalar", wdep)
    lnctx2 = {}
    for ch in range(4):
        c0 = ch * 512
        ld = p.dma("gpsimd", x1c, r(x1p_d)[:, :, c0:c0 + 512])
        p.wait("vector", ld)
        for m in range(8):
            p.run("vector", lambda e, _m=m, _c=c0: e.tensor_add(
                x1c[:, _m, :], x1c[:, _m, :], h_sb[:, _m, _c:_c + 512]))
            vdep = p.run("vector", lambda e, _m=m: e.tensor_copy(x1b[:, _m, :], x1c[:, _m, :]), mark=True)
        p.wait("sync", vdep)
        p.wait("tensor", vdep)
        p.dma("sync", r(X1)[:, :, c0:c0 + 512], x1c, "os")

        def out_cb2(k, bnm, brs):
            t1 = t1s[k % 2]
            p.run("vector", lambda e, _k=k: e.tensor_add(t1, x1c[:, _k, :], bnm))
            vdep = p.run("vector", lambda e, _k=k: e.tensor_tensor(
                xn1b[:, _k, :], t1, brs, ALU.mult), mark=True)
            return vdep
        lndep = emit_ln(p, lnctx2, xbf=lambda k: x1b[:, k, :], nk=8, ncol=512,
                        ones_bf=ones_bf, ones_f=ones_f, sq2=sq2, stat=stat, out_cb=out_cb2,
                        xdep=vdep)
        p.wait("sync", lndep)
        p.dma("sync", r(XN1)[:, :, c0:c0 + 512], xn1b, "os")
        p.wait("tensor", lndep)

        def evacCQ(m, chx, ps, dep):
            p.wait("scalar", dep)
            return p.run("scalar", lambda e, _m=m: e.activation(
                cqb[:, _m, :], ps, AF.Identity, bias=bq_sb[:, _m:_m + 1]), mark=True)
        emit_linear(p, nk=8, nm=4, chunks=[(0, 512)],
                    wf=lambda k, m: wq_sb[:, k * 512 + m * 128:k * 512 + (m + 1) * 128],
                    xf=lambda k, _c: xn1b[:, k, :], evac=evacCQ)
        sS = p.sem("ss")
        p.wait("sync", (sS, sS.n))
        p.dma("sync", CQ.rearrange("(m p) t -> p m t", p=128)[:, :, c0:c0 + 512], cqb, "os")

        def evacKV(m, chx, ps, dep, _c0=c0):
            p.wait("scalar", dep)
            sdep = p.run("scalar", lambda e: e.activation(
                x1b[0:64, 0, :], ps[0:64, :], AF.Identity, bias=bkv_sb), mark=True)
            p.wait("sync", sdep)
            p.dma("sync", CKV[:, _c0:_c0 + 512], x1b[0:64, 0, :], "os")
            return sdep
        emit_linear(p, nk=8, nm=1, chunks=[(0, 512)],
                    wf=lambda k, m: wkv_sb[:, k * 64:(k + 1) * 64],
                    xf=lambda k, _c: xn1b[:, k, :], evac=evacKV)
        chunk_barrier(p)
    st2.close()
    return p


def prep_l0post(ins, outs1, outs2, c):
    b, half = c // 2, c % 2
    t0 = half * TC
    x = np.asarray(ins["L0_attn_x"])
    o1 = outs1[c]
    yre = np.concatenate([outs2[2 * b]["Y"][:, t0:t0 + TC], outs2[2 * b + 1]["Y"][:, t0:t0 + TC]], axis=0)
    qT = np.stack([np.ascontiguousarray(np.asarray(ins["L0_ffn_qweights"])[i].T) for i in range(6)])
    scT = np.stack([np.repeat(np.asarray(ins["L0_ffn_scales"], np.float32)[i].T, 16, axis=0) for i in range(6)])
    qb = np.stack([col_tiles(np.asarray(ins["L0_ffn_biases"])[i]) for i in range(6)])
    laT = np.stack([_f32(np.asarray(ins["L0_ffn_lora_a"])[i].T) for i in range(6)])
    lbT = np.stack([_f32(np.asarray(ins["L0_ffn_lora_b"])[i].T) for i in range(6)])
    lng = np.stack([col_tiles(np.asarray(ins["L0_ffn_ln_gamma"])[i]) for i in range(3)])
    lnb = np.stack([col_tiles(np.asarray(ins["L0_ffn_ln_beta"])[i]) for i in range(3)])
    return {
        "x_sh": _f32(x[b, t0:t0 + TC].T), "XNi": o1["XN"], "LEFTCi": o1["LEFTC"],
        "GATEi": o1["GATE"], "Yre": _f32(yre), "dcol": col_tiles(ins["L0_attn_D_ssm"]),
        "w_out": _f32(ins["L0_attn_w_out"]), "b_out": col_tiles(ins["L0_attn_b_out"]),
        "qT": qT.astype(np.int32), "scT": scT, "qb": qb, "laT": laT, "lbT": lbT,
        "lng": lng, "lnb": lnb,
        "wq_dT": _f32(np.asarray(ins["L1_attn_wq_d"]).T), "bq_d": col_tiles(ins["L1_attn_bq_d"]),
        "wkv_dT": _f32(np.asarray(ins["L1_attn_wkv_d"]).T),
        "bkv_d": _f32(np.asarray(ins["L1_attn_bkv_d"])[:, None]),
    }


def build_l1ffn(p=None, io=None):
    p = p or Prog("l1ffn")
    io = io or ExtIO(p)
    XN1i = io.inp("XN1i", [D, TC], BF16)
    w_router = io.inp("w_router", [128, 8, 512])
    w_up = io.inp("w_up", [7, D, 4096])
    w_gate = io.inp("w_gate", [7, D, 4096])
    w_down = io.inp("w_down", [7, 4096, D])
    ws_up = io.inp("ws_up", [D, 4096])
    ws_gate = io.inp("ws_gate", [D, 4096])
    ws_down = io.inp("ws_down", [4096, D])
    FFN1 = io.out("FFN1", [TC, D])
    MASKD2 = io.out("MASKD2", [128, 128])

    p.regconsts(1.0, -1.0, 2.0)
    zeros8, zdep = p.const_sbuf(np.zeros((128, 8), np.float32))
    p.psum_pool(8)

    xn1_sb = p.sbuf("xn1_sb", [128, 8, TC], BF16)
    wr_sb = p.sbuf("wr_sb", [128, 8, 512], BF16)
    mask_tm = p.sbuf("mask_tm", [128, 16, 8], F32)
    g_f = p.sbuf("g_f", [128, 8], F32)
    cum = p.sbuf("cum", [128, 8], F32)
    scol = p.sbuf("scol", [128, 8], F32)
    he = p.sbuf("he", [128, 32, 1024], BF16)
    wd_sb = p.sbuf("wd_sb", [128, 32, 512], BF16)
    wu_blk = p.sbuf("wu_blk", [128, 8, 8, 128], BF16)
    wg_blk = p.sbuf("wg_blk", [128, 8, 8, 128], BF16)
    acc = p.sbuf("acc", [128, 8, 1024], F32)
    t1s = [p.sbuf("t1x", [128, 512], BF16), p.sbuf("t1y", [128, 512], BF16)]

    xdep = p.dma("sync", xn1_sb, XN1i.rearrange("(m p) t -> p m t", p=128))
    wrdep = p.dma("gpsimd", wr_sb, w_router)
    p.wait("tensor", [xdep, wrdep])
    p.wait("vector", [xdep, zdep])

    ex_t = p.sbuf("ex_t", [128, 8], F32)
    prb = p.sbuf("prb", [128, 8], F32)
    m1c = p.sbuf("m1c", [128, 8], F32)
    csc = p.sbuf("csc", [128, 8], F32)
    e1c = p.sbuf("e1c", [128, 8], F32)
    for tt in range(16):
        i, ps = p.psum_acquire()
        tiles = [(xn1_sb[:, k, tt * 128:(tt + 1) * 128], wr_sb[:, k, :], 128, 512) for k in range(8)]
        mdep = emit_matmul_group(p, ps, tiles)
        p.wait("scalar", mdep)
        s1 = p.run("scalar", lambda e, _ps=ps: e.copy(g_f, _ps[:, 0:8]), mark=True)
        p.wait("vector", s1)
        v1 = p.run("vector", lambda e: e.reduce_max(scol[:, 0:1], g_f, axis=AX.X, negate=True), mark=True)
        p.wait("scalar", v1)
        s2 = p.run("scalar", lambda e: e.activation(ex_t, g_f, AF.Exp, bias=scol[:, 0:1],
                                                    accum_out=scol[:, 1:2]), mark=True)
        p.psum_release(i, s1)
        p.wait("vector", s2)
        v2 = p.run("vector", lambda e: e.reciprocal(scol[:, 2:3], scol[:, 1:2]), mark=True)
        p.wait("scalar", v2)
        s3 = p.run("scalar", lambda e: e.activation(prb, ex_t, AF.Identity, scale=scol[:, 2:3]), mark=True)
        p.wait("vector", s3)
        v3 = p.run("vector", lambda e: e.reduce_max(scol[:, 3:4], prb, axis=AX.X), mark=True)
        p.wait("gpsimd", v3)
        g1 = p.run("gpsimd", lambda e: e.tensor_scalar(m1c, prb, scol[:, 3:4], None, ALU.is_equal), mark=True)
        p.wait("vector", g1)
        v4 = p.run("vector", lambda e: e.tensor_tensor_scan(csc, m1c, zeros8, 0.0, ALU.add, ALU.add), mark=True)
        p.wait("gpsimd", v4)
        g2 = p.run("gpsimd", lambda e: e.tensor_scalar(e1c, csc, 1.0, None, ALU.is_equal), mark=True)
        p.wait("scalar", v3)
        s4 = p.run("scalar", lambda e: e.mul(scol[:, 4:5], scol[:, 3:4], 2.0), mark=True)
        p.wait("vector", g2)
        v5 = p.run("vector", lambda e: e.tensor_mul(e1c, m1c, e1c), mark=True)
        p.wait("gpsimd", [v5, s4])
        g3 = p.run("gpsimd", lambda e, _tt=tt: e.tensor_scalar(
            mask_tm[:, _tt, :], e1c, scol[:, 4:5], None, ALU.mult), mark=True)
        p.wait("vector", g3)
        p.wait("scalar", g3)
    vS = p.sem("vs")
    p.wait("sync", (vS, vS.n))
    p.dma("sync", MASKD2, mask_tm.rearrange("p t e -> p (t e)"), "os")
    p.barrier()

    r3 = lambda ap: ap.rearrange("(k p) c -> p k c", p=128)
    r4 = lambda ap: ap.rearrange("(k p) (m c) -> p k m c", p=128, c=128)
    FFN1_r = FFN1.rearrange("(tb p) d -> p tb d", p=128)
    stdep = None
    for th in range(2):
        if stdep is not None:
            p.wait("vector", stdep)
        for si in range(8):
            e_i = si - 1
            tS = p.sem("ts")
            p.wait("vector", (tS, tS.n))
            hedeps = []
            for mb in range(4):
                tS = p.sem("ts")
                p.wait("gpsimd", (tS, tS.n))
                if si == 0:
                    lu = p.dma("gpsimd", wu_blk, r4(ws_up[:, mb * 1024:(mb + 1) * 1024]))
                    lg = p.dma("gpsimd", wg_blk, r4(ws_gate[:, mb * 1024:(mb + 1) * 1024]))
                else:
                    lu = p.dma("gpsimd", wu_blk, r4(w_up[e_i][:, mb * 1024:(mb + 1) * 1024]))
                    lg = p.dma("gpsimd", wg_blk, r4(w_gate[e_i][:, mb * 1024:(mb + 1) * 1024]))
                p.wait("tensor", lg)
                for m8 in range(8):
                    m = mb * 8 + m8
                    for ch in range(2):
                        cols = th * 1024 + ch * 512
                        iu, psu = p.psum_acquire()
                        du = emit_matmul_group(p, psu, [
                            (wu_blk[:, k, m8, :], xn1_sb[:, k, cols:cols + 512], 128, 512) for k in range(8)])
                        ig, psg = p.psum_acquire()
                        dg = emit_matmul_group(p, psg, [
                            (wg_blk[:, k, m8, :], xn1_sb[:, k, cols:cols + 512], 128, 512) for k in range(8)])
                        t1 = t1s[(m * 2 + ch) % 2]
                        p.wait("scalar", dg)
                        if len(hedeps) >= 2:
                            p.wait("scalar", hedeps[-2])
                        sdep = p.run("scalar", lambda e, _t=t1, _ps=psg: e.activation(_t, _ps, AF.Silu), mark=True)
                        p.wait("vector", sdep)
                        vdep = p.run("vector", lambda e, _t=t1, _ps=psu, _m=m, _ch=ch: e.tensor_tensor(
                            he[:, _m, _ch * 512:(_ch + 1) * 512], _t, _ps, ALU.mult), mark=True)
                        hedeps.append(vdep)
                        p.psum_release(iu, vdep)
                        p.psum_release(ig, vdep)
            p.wait("tensor", hedeps[-1])
            for dn in range(2):
                tS = p.sem("ts")
                p.wait("gpsimd", (tS, tS.n))
                if si == 0:
                    ld = p.dma("gpsimd", wd_sb, r3(ws_down[:, dn * 512:(dn + 1) * 512]))
                else:
                    ld = p.dma("gpsimd", wd_sb, r3(w_down[e_i][:, dn * 512:(dn + 1) * 512]))
                p.wait("tensor", ld)
                for tt8 in range(8):
                    ttg = th * 8 + tt8
                    i2, ps2 = p.psum_acquire()
                    dd = emit_matmul_group(p, ps2, [
                        (he[:, mk, tt8 * 128:(tt8 + 1) * 128], wd_sb[:, mk, :], 128, 512) for mk in range(32)])
                    p.wait("vector", dd)
                    if si == 0:
                        cdep = p.run("vector", lambda e, _t=tt8, _d=dn, _ps=ps2: e.tensor_copy(
                            acc[:, _t, _d * 512:(_d + 1) * 512], _ps), mark=True)
                    else:
                        cdep = p.run("vector", lambda e, _t=tt8, _d=dn, _ps=ps2, _tg=ttg, _e=e_i:
                                     e.scalar_tensor_tensor(
                                         acc[:, _t, _d * 512:(_d + 1) * 512], _ps,
                                         mask_tm[:, _tg, _e:_e + 1],
                                         acc[:, _t, _d * 512:(_d + 1) * 512], ALU.mult, ALU.add), mark=True)
                    p.psum_release(i2, cdep)
        vS = p.sem("vs")
        p.wait("sync", (vS, vS.n))
        stdep = p.dma("sync", FFN1_r[:, th * 8:(th + 1) * 8, :], acc, "os")
    chunk_barrier(p)
    return p


def prep_l1ffn(ins, outs3, c):
    return {
        "XN1i": outs3[c]["XN1"],
        "w_router": _f32(np.tile(np.asarray(ins["L1_ffn_w_router"], np.float32)
                                  .reshape(8, 128, 8).transpose(1, 0, 2), (1, 1, 64))),
        "w_up": _f32(ins["L1_ffn_w_up"]), "w_gate": _f32(ins["L1_ffn_w_gate"]),
        "w_down": _f32(ins["L1_ffn_w_down"]),
        "ws_up": _f32(ins["L1_ffn_ws_up"]), "ws_gate": _f32(ins["L1_ffn_ws_gate"]),
        "ws_down": _f32(ins["L1_ffn_ws_down"]),
    }


def build_l1attn(p=None, io=None):
    p = p or Prog("l1attn")
    io = io or ExtIO(p)
    SQ = 4096
    CQb = io.inp("CQb", [512, SQ], BF16)
    CKVb = io.inp("CKVb", [64, SQ], BF16)
    ckvT1 = io.inp("ckvT1", [128, 32, 72], BF16)
    w_qkT = io.inp("w_qkT", [512, 512])
    b_qk = io.inp("b_qk", [64, 8])
    wv_uT = io.inp("wv_uT", [64, 512])
    bv_u = io.inp("bv_u", [64, 8])
    YH = io.out("YH", [512, SQ], BF16)

    p.regconsts(-30.0, 0.125)
    ones_f, of_dep = p.const_sbuf(np.ones((1, 128), dtype=np.float32))
    p.psum_pool(6)
    ps_av = p.psum_t("ps_av", [128, 512], F32)
    ps_y = p.psum_t("ps_y", [128, 512], F32)

    cq_sb = p.sbuf("cq_sb", [128, 4, SQ], BF16)
    q_sb = p.sbuf("q_sb", [64, 8, SQ], BF16)
    ckv_sb = p.sbuf("ckv_sb", [64, SQ], BF16)
    ckvt_sb = p.sbuf("ckvt_sb", [128, 32, 72], BF16)
    wqk_sb = p.sbuf("wqk_sb", [128, 4, 512], BF16)
    bqk_sb = p.sbuf("bqk_sb", [64, 8], F32)
    wvu_sb = p.sbuf("wvu_sb", [64, 512], BF16)
    bvu_sb = p.sbuf("bvu_sb", [64, 8], F32)
    pb = [p.sbuf("pb0", [128, 512], BF16), p.sbuf("pb1", [128, 512], BF16)]
    av_sb = p.sbuf("av_sb", [64, 512], BF16)
    rl_sb = p.sbuf("rl_sb", [1, 512], F32)
    rlb_sb = p.sbuf("rlb_sb", [64, 512], F32)
    yv_sb = p.sbuf("yv_sb", [64, 512], F32)
    yo_sb = p.sbuf("yo_sb", [64, 512], BF16)

    d1 = p.dma("sync", cq_sb, CQb.rearrange("(m p) t -> p m t", p=128))
    d2 = p.dma("sync", ckv_sb, CKVb)
    d3 = p.dma("sync", ckvt_sb, ckvT1)
    d4 = p.dma("gpsimd", wqk_sb, w_qkT.rearrange("(k p) o -> p k o", p=128))
    d5 = p.dma("sync", bqk_sb, b_qk)
    d6 = p.dma("gpsimd", wvu_sb, wv_uT)
    d7 = p.dma("sync", bvu_sb, bv_u)
    p.wait("tensor", [d1, d4, of_dep])
    p.wait("scalar", [d5, d7])

    def evacQ(m, ch, ps, dep, ):
        p.wait("scalar", dep)
        p.run("scalar", lambda e, _m=m, _c=ch: e.activation(
            q_sb[:, 2 * _m, _c * 512:(_c + 1) * 512], ps[0:64, :], AF.Identity,
            bias=bqk_sb[0:64, 2 * _m:2 * _m + 1]), mark=True)
        return p.run("scalar", lambda e, _m=m, _c=ch: e.activation(
            q_sb[:, 2 * _m + 1, _c * 512:(_c + 1) * 512], ps[64:128, :], AF.Identity,
            bias=bqk_sb[0:64, 2 * _m + 1:2 * _m + 2]), mark=True)
    emit_linear(p, nk=4, nm=4, chunks=[(c, 512) for c in range(8)],
                wf=lambda k, m: wqk_sb[:, k, m * 128:(m + 1) * 128],
                xf=lambda k, c: cq_sb[:, k, c * 512:(c + 1) * 512], evac=evacQ)
    sS = p.sem("ss")
    p.wait("tensor", (sS, sS.n))
    p.wait("tensor", [d2, d3])
    p.barrier()

    ydeps = []
    YH_h = YH.rearrange("(hh q) t -> q hh t", q=64)
    prev_clear = None
    prev_store = None
    for h in range(8):
        for tch in range(8):
            qh = q_sb[:, h, tch * 512:(tch + 1) * 512]
            pdep = [None, None]
            if prev_clear is not None:
                p.wait("tensor", prev_clear)
            for st in range(32):
                i1, ps1 = p.psum_acquire()
                sdep = emit_matmul_group(p, ps1, [(ckv_sb[:, st * 128:(st + 1) * 128], qh, 128, 512)])
                buf = pb[st % 2]
                p.wait("scalar", sdep)
                if pdep[st % 2] is not None:
                    p.wait("scalar", pdep[st % 2])
                edep = p.run("scalar", lambda e, _b=buf, _ps=ps1: e.activation(
                    _b, _ps, AF.Exp, scale=0.125, bias=-30.0), mark=True)
                p.psum_release(i1, edep)
                p.wait("tensor", edep)
                avdep = p.run("tensor", lambda e, _st=st, _b=buf: e.matmul(
                    ps_av[0:72, 0:512], ckvt_sb[:, _st, :], _b,
                    start=(_st == 0), stop=(_st == 31)), mark=(st == 31))
                tS = p.sem("ts")
                pdep[st % 2] = (tS, tS.n)
            p.wait("scalar", avdep)
            if prev_store is not None:
                p.wait("scalar", prev_store)
            adep = p.run("scalar", lambda e: e.copy(av_sb, ps_av[0:64, :]), mark=True)
            p.wait("vector", avdep)
            rdep = p.run("vector", lambda e: e.reciprocal(rl_sb, ps_av[64:65, :]), mark=True)
            p.wait("tensor", [adep, rdep])
            ydep = p.run("tensor", lambda e, _h=h: e.matmul(
                ps_y[0:64, 0:512], wvu_sb[:, _h * 64:(_h + 1) * 64], av_sb,
                start=True, stop=True), mark=True)
            bdep = p.run("tensor", lambda e: e.matmul(
                ps_av[0:64, 0:512], ones_f[0:1, 0:64], rl_sb, start=True, stop=True), mark=True)
            p.wait("vector", bdep)
            vdep = p.run("vector", lambda e: e.tensor_copy(rlb_sb, ps_av[0:64, :]), mark=True)
            p.wait("vector", ydep)
            vdep2 = p.run("vector", lambda e: e.tensor_tensor(yv_sb, ps_y[0:64, :], rlb_sb, ALU.mult), mark=True)
            p.wait("scalar", vdep2)
            odep = p.run("scalar", lambda e, _h=h: e.activation(
                yo_sb, yv_sb, AF.Identity, bias=bvu_sb[:, _h:_h + 1]), mark=True)
            p.wait("sync", odep)
            ydeps.append(p.dma("sync", YH_h[:, h, tch * 512:(tch + 1) * 512], yo_sb, "os"))
            prev_clear = vdep
            prev_store = ydeps[-1]
    p.wait("sync", ydeps[-1])
    chunk_barrier(p)
    return p


def prep_l1attn(ins, outs3, c):
    b, hg = c // 2, c % 2
    cq = np.concatenate([outs3[2 * b]["CQ"], outs3[2 * b + 1]["CQ"]], axis=1)
    ckv = np.concatenate([outs3[2 * b]["CKV"], outs3[2 * b + 1]["CKV"]], axis=1)
    ckvt = np.ascontiguousarray(ckv.T)
    ckvt1 = np.concatenate([ckvt, np.ones((S, 8), dtype=BF)], axis=1)
    ckvt1 = np.ascontiguousarray(ckvt1.reshape(32, 128, 72).transpose(1, 0, 2))
    wqkT = np.asarray(ins["L1_attn_w_qk"], np.float32).T
    wqkT = np.ascontiguousarray(wqkT[:, hg * 512:(hg + 1) * 512])
    bqk = np.ascontiguousarray(np.asarray(ins["L1_attn_b_qk"], np.float32)[hg * 512:(hg + 1) * 512].reshape(8, 64).T)
    wvuT = np.asarray(ins["L1_attn_wv_u"], np.float32).T
    wvuT = np.ascontiguousarray(wvuT[:, hg * 512:(hg + 1) * 512])
    bvu = np.asarray(ins["L1_attn_bv_u"], np.float32)[hg * 512:(hg + 1) * 512]
    bvu = np.ascontiguousarray(bvu.reshape(8, 64).T)
    return {"CQb": cq, "CKVb": ckv, "ckvT1": ckvt1, "w_qkT": wqkT, "b_qk": bqk,
            "wv_uT": wvuT, "bv_u": bvu}


def build_l2pre(p=None, io=None):
    p = p or Prog("l2pre")
    io = io or ExtIO(p)
    X1i = io.inp("X1i", [D, TC])
    YHa = io.inp("YHa", [D, TC], BF16)
    FFN1f = io.inp("FFN1f", [D, TC])
    w_projT = io.inp("w_projT", [D, D])
    b_proj = io.inp("b_proj", [128, 8])
    Wq = io.inp("Wq", [D, D])
    Wk = io.inp("Wk", [D, D])
    Wv = io.inp("Wv", [D, 2048])
    Wb = io.inp("Wb", [D, 4])
    wgT = io.inp("wgT", [D, 3072], I32)
    wuT = io.inp("wuT", [D, 3072], I32)
    wdT = io.inp("wdT", [3072, D], I32)

    X2 = io.out("X2", [D, TC])
    XN2O = io.out("XN2O", [D, TC])
    QP = io.out("QP", [D, TC], BF16)
    KP = io.out("KP", [D, TC], BF16)
    VP = io.out("VP", [2048, TC], BF16)
    BETA = io.out("BETA", [4, TC])
    FFN2 = io.out("FFN2", [D, TC])

    p.regconsts(EPS, 1.0, -1.0, -0.5, 1.0 / 127.0)
    ones_bf, ob_dep = p.const_sbuf(np.ones((128, 1), dtype=BF))
    ones_f, of_dep = p.const_sbuf(np.ones((1, 128), dtype=np.float32))
    p.wait("tensor", [ob_dep, of_dep])
    p.psum_pool(8)
    r = lambda ap: ap.rearrange("(m p) t -> p m t", p=128)

    wq_sb = p.sbuf("wq_sb", [128, 8, 1024], BF16)
    wk_sb = p.sbuf("wk_sb", [128, 8, 1024], BF16)
    wb_sb = p.sbuf("wb_sb", [128, 8, 4], BF16)
    xnb = p.sbuf("xnb", [128, 8, TC], BF16)
    sq2 = [p.sbuf("sq2a", [128, 512], BF16), p.sbuf("sq2b", [128, 512], BF16)]
    stat = p.sbuf("stat", [1, 2048], F32)
    t1s = [p.sbuf("t1a", [128, 512], F32), p.sbuf("t1b", [128, 512], F32)]
    stA = ExitStack()
    sbA = lambda nm, shp, dt_: stA.enter_context(p.nc.sbuf_tensor(nm, list(shp), dt_))[:]
    wp_sb = sbA("wp_sb", [128, 8, 1024], BF16)
    bp_sb = sbA("bp_sb", [128, 8], F32)
    x1c = sbA("x1c", [128, 8, 512], F32)
    yhc = sbA("yhc", [128, 8, 512], BF16)
    f1c = sbA("f1c", [128, 8, 512], F32)
    x2c = sbA("x2c", [128, 8, 512], F32)
    x2b = sbA("x2b", [128, 8, 512], BF16)
    xnc = sbA("xnc", [128, 8, 512], F32)

    wdeps = []
    for k in range(8):
        wdeps.append(p.dma("gpsimd", wp_sb[:, k, :], w_projT[k * 128:(k + 1) * 128, :]))
        wdeps.append(p.dma("gpsimd", wq_sb[:, k, :], Wq[k * 128:(k + 1) * 128, :]))
        wdeps.append(p.dma("gpsimd", wk_sb[:, k, :], Wk[k * 128:(k + 1) * 128, :]))
        wdeps.append(p.dma("gpsimd", wb_sb[:, k, :], Wb[k * 128:(k + 1) * 128, :]))
    wdeps.append(p.dma("sync", bp_sb, b_proj))
    p.wait("tensor", wdeps[-2])
    p.wait("tensor", wdeps[-1])
    p.wait("scalar", wdeps[-1])

    lnctx = {}
    for ch in range(4):
        c0 = ch * 512
        p.dma("sync", x1c, r(X1i)[:, :, c0:c0 + 512])
        p.dma("sync", yhc, r(YHa.rearrange("(m p) t -> p m t", p=128) if False else YHa)[:, :, c0:c0 + 512] if False else r(YHa)[:, :, c0:c0 + 512])
        ld = p.dma("sync", f1c, r(FFN1f)[:, :, c0:c0 + 512])
        p.wait("tensor", ld)
        p.wait("vector", ld)

        def evacP(m, chx, ps, dep):
            p.wait("scalar", dep)
            sdep = p.run("scalar", lambda e, _m=m: e.activation(
                x2c[:, _m, :], ps, AF.Identity, bias=bp_sb[:, _m:_m + 1]), mark=True)
            p.wait("vector", sdep)
            p.run("vector", lambda e, _m=m: e.tensor_add(x2c[:, _m, :], x2c[:, _m, :], x1c[:, _m, :]))
            p.run("vector", lambda e, _m=m: e.tensor_add(x2c[:, _m, :], x2c[:, _m, :], f1c[:, _m, :]))
            return p.run("vector", lambda e, _m=m: e.tensor_copy(x2b[:, _m, :], x2c[:, _m, :]), mark=True)
        emit_linear(p, nk=8, nm=8, chunks=[(0, 512)],
                    wf=lambda k, m: wp_sb[:, k, m * 128:(m + 1) * 128],
                    xf=lambda k, _c: yhc[:, k, :], evac=evacP)
        vS = p.sem("vs")
        p.wait("sync", (vS, vS.n))
        p.dma("sync", r(X2)[:, :, c0:c0 + 512], x2c, "os")

        def out_cb(k, bnm, brs, _c0=c0):
            t1 = t1s[k % 2]
            p.run("vector", lambda e, _k=k: e.tensor_add(t1, x2c[:, _k, :], bnm))
            vdep = p.run("vector", lambda e, _k=k, _t=t1: e.tensor_tensor(xnc[:, _k, :], _t, brs, ALU.mult), mark=True)
            vdep2 = p.run("vector", lambda e, _k=k, _c=_c0: e.tensor_copy(xnb[:, _k, _c:_c + 512], xnc[:, _k, :]), mark=True)
            return vdep2
        lndep = emit_ln(p, lnctx, xbf=lambda k: x2b[:, k, :], nk=8, ncol=512,
                        ones_bf=ones_bf, ones_f=ones_f, sq2=sq2, stat=stat,
                        out_cb=out_cb, xdep=None)
        p.wait("sync", lndep)
        p.dma("sync", r(XN2O)[:, :, c0:c0 + 512], xnc, "os")
        chunk_barrier(p)

    stA.close()
    stB = ExitStack()
    sbB = lambda nm, shp, dt_: stB.enter_context(p.nc.sbuf_tensor(nm, list(shp), dt_))[:]
    wv_sb = sbB("wv_sb", [128, 8, 2048], BF16)
    stg = sbB("stg", [128, 8, 512], BF16)
    stg2 = sbB("stg2", [128, 16, 512], BF16)
    bstg = sbB("bstg", [4, TC], F32)
    wvd = []
    for k in range(8):
        wvd.append(p.dma("gpsimd", wv_sb[:, k, :], Wv[k * 128:(k + 1) * 128, :]))
    p.wait("tensor", wvd[-1])
    for name, wsb, nm_, out in (("q", wq_sb, 8, QP), ("k", wk_sb, 8, KP), ("v", wv_sb, 16, VP)):
        for ch in range(4):
            sdeps = []
            def evacS(m, chx, ps, dep, _s=(stg if nm_ == 8 else stg2)):
                p.wait("scalar", dep)
                d = p.run("scalar", lambda e, _m=m: e.activation(_s[:, _m, :], ps, AF.Silu), mark=True)
                sdeps.append(d)
                return d
            emit_linear(p, nk=8, nm=nm_, chunks=[(ch, 512)],
                        wf=lambda k, m, _w=wsb: _w[:, k, m * 128:(m + 1) * 128],
                        xf=lambda k, c: xnb[:, k, c * 512:(c + 1) * 512], evac=evacS)
            p.wait("sync", sdeps[-1])
            sdep2 = p.dma("sync", out.rearrange("(m p) t -> p m t", p=128)[:, :, ch * 512:(ch + 1) * 512],
                          (stg if nm_ == 8 else stg2)[:, 0:nm_, :], "os")
            p.wait("scalar", sdep2)
    for ch in range(4):
        def evacB(m, chx, ps, dep):
            p.wait("scalar", dep)
            return p.run("scalar", lambda e, _c=chx: e.activation(
                bstg[:, _c * 512:(_c + 1) * 512], ps[0:4, :], AF.Sigmoid), mark=True)
        emit_linear(p, nk=8, nm=1, chunks=[(ch, 512)],
                    wf=lambda k, m: wb_sb[:, k, :],
                    xf=lambda k, c: xnb[:, k, c * 512:(c + 1) * 512], evac=evacB)
    sS = p.sem("ss")
    p.wait("sync", (sS, sS.n))
    p.dma("sync", BETA, bstg, "os")
    chunk_barrier(p)

    stB.close()
    stf = ExitStack()
    sbf = lambda nm, shp, dt_: stf.enter_context(p.nc.sbuf_tensor(nm, list(shp), dt_))[:]
    wd_l2 = sbf("wd_l2", [128, 24, 1024], BF16)
    wgm = [sbf("wgm0", [128, 8, 128], BF16), sbf("wgm1", [128, 8, 128], BF16)]
    wum = [sbf("wum0", [128, 8, 128], BF16), sbf("wum1", [128, 8, 128], BF16)]
    hl2 = sbf("hl2", [128, 24, 512], BF16)
    f2c = sbf("f2c", [128, 8, 512], F32)
    sil = [sbf("sil0", [128, 512], BF16), sbf("sil1", [128, 512], BF16)]
    wdd = []
    for k in range(24):
        wdd.append(p.dma("gpsimd", wd_l2[:, k, :], wdT[k * 128:(k + 1) * 128, :]))
    for ch in range(4):
        c0 = ch * 512
        hdeps = []
        for m in range(24):
            pr = m % 2
            tS = p.sem("ts")
            p.wait("gpsimd", (tS, tS.n))
            lg = p.dma("gpsimd", wgm[pr], wgT[:, m * 128:(m + 1) * 128].rearrange("(k p) c -> p k c", p=128))
            lu = p.dma("gpsimd", wum[pr], wuT[:, m * 128:(m + 1) * 128].rearrange("(k p) c -> p k c", p=128))
            p.wait("tensor", lu)
            ig, psg = p.psum_acquire()
            dg = emit_matmul_group(p, psg, [(wgm[pr][:, k, :], xnb[:, k, c0:c0 + 512], 128, 512) for k in range(8)])
            iu, psu = p.psum_acquire()
            du = emit_matmul_group(p, psu, [(wum[pr][:, k, :], xnb[:, k, c0:c0 + 512], 128, 512) for k in range(8)])
            t1 = sil[m % 2]
            p.wait("scalar", dg)
            if len(hdeps) >= 2:
                p.wait("scalar", hdeps[-2])
            sdep = p.run("scalar", lambda e, _t=t1, _ps=psg: e.activation(_t, _ps, AF.Silu, scale=1.0 / 127.0), mark=True)
            p.wait("vector", [sdep, du])
            vdep = p.run("vector", lambda e, _t=t1, _ps=psu, _m=m: e.scalar_tensor_tensor(
                hl2[:, _m, :], _ps, 1.0 / 127.0, _t, ALU.mult, ALU.mult), mark=True)
            hdeps.append(vdep)
            p.psum_release(ig, vdep)
            p.psum_release(iu, vdep)
        p.wait("tensor", hdeps[-1])
        p.wait("tensor", wdd[-1])
        def evacF(m, chx, ps, dep):
            p.wait("scalar", dep)
            return p.run("scalar", lambda e, _m=m: e.activation(
                f2c[:, _m, :], ps, AF.Identity, scale=1.0 / 127.0), mark=True)
        emit_linear(p, nk=24, nm=8, chunks=[(0, 512)],
                    wf=lambda k, m: wd_l2[:, k, m * 128:(m + 1) * 128],
                    xf=lambda k, _c: hl2[:, k, :], evac=evacF)
        sS = p.sem("ss")
        p.wait("sync", (sS, sS.n))
        p.dma("sync", r(FFN2)[:, :, c0:c0 + 512], f2c, "os")
        chunk_barrier(p)
    stf.close()
    return p


def prep_l2pre(ins, outs3, outs4, outs3b, c):
    b, hf = c // 2, c % 2
    t0 = hf * TC
    yh = np.concatenate([outs4[2 * b]["YH"][:, t0:t0 + TC], outs4[2 * b + 1]["YH"][:, t0:t0 + TC]], axis=0)
    f1 = np.ascontiguousarray(outs3b[c]["FFN1"].T)
    return {
        "X1i": outs3[c]["X1"], "YHa": np.ascontiguousarray(yh), "FFN1f": _f32(f1),
        "w_projT": _f32(np.asarray(ins["L1_attn_w_proj"]).T), "b_proj": col_tiles(ins["L1_attn_b_proj"]),
        "Wq": _f32(ins["L2_attn_Wq"]), "Wk": _f32(ins["L2_attn_Wk"]),
        "Wv": _f32(ins["L2_attn_Wv"]), "Wb": _f32(ins["L2_attn_Wb"]),
        "wgT": np.ascontiguousarray(np.asarray(ins["L2_ffn_w_gate"]).T.astype(np.int32)),
        "wuT": np.ascontiguousarray(np.asarray(ins["L2_ffn_w_up"]).T.astype(np.int32)),
        "wdT": np.ascontiguousarray(np.asarray(ins["L2_ffn_w_down"]).T.astype(np.int32)),
    }


def build_l2attn(p=None, io=None):
    p = p or Prog("l2attn")
    io = io or ExtIO(p)
    NU = 2
    QPh = io.inp("QPh", [NU, 256, S], BF16)
    KPh = io.inp("KPh", [NU, 256, S], BF16)
    VT16 = io.inp("VT16", [NU, 16, 256, 512], BF16)
    BETAc = io.inp("BETAc", [NU, 16, 256])
    BETAr = io.inp("BETAr", [NU, 1, S])
    onw = io.inp("onw", [128, 512])
    ON = io.out("ON", [NU, S, 512], BF16)
    DBG = io.out("DBG", [16, 2048], F32)

    p.regconsts(1e-6, 1e-5, 1.0, -1.0, -0.5, 1.0 / 512.0, -2.7725887222397811)
    idm = np.zeros((16, 128), dtype=BF)
    idm[:, 0:16] = np.eye(16, dtype=BF)
    ident16, i16_dep = p.const_sbuf(idm)
    ident, id_dep = p.const_sbuf(np.eye(128, dtype=BF))
    lt = np.zeros((16, 16), dtype=BF)
    for i in range(16):
        lt[i, :i] = 1
    ltm, lt_dep = p.const_sbuf(lt)
    uts = np.zeros((16, 16), dtype=BF)
    for j in range(16):
        uts[j, j + 1:] = 1
    utsm, uts_dep = p.const_sbuf(uts)
    utst = np.tile(np.zeros((16, 16), dtype=np.float32), (1, 8))
    for j in range(16):
        for w in range(8):
            utst[j, w * 16 + j + 1:(w + 1) * 16] = -1.0
    utsT_tiled, utt_dep = p.const_sbuf(utst.astype(BF))
    uti = np.zeros((16, 16), dtype=BF)
    for j in range(16):
        uti[j, j:] = 1
    utim, uti_dep = p.const_sbuf(uti)
    ones_bf, ob_dep = p.const_sbuf(np.ones((128, 1), dtype=BF))
    ones_f, of_dep = p.const_sbuf(np.ones((1, 128), dtype=np.float32))
    p.wait("tensor", [i16_dep, id_dep, lt_dep, uts_dep, uti_dep, ob_dep, of_dep, utt_dep])
    p.psum_pool(6)
    psT = p.psum_t("psT", [128, 1024], BF16)

    q_sb = p.sbuf("q_sb", [128, 2, S], BF16)
    k_sb = p.sbuf("k_sb", [128, 2, S], BF16)
    vt_sb = p.sbuf("vt_sb", [16, 8, 512], BF16)
    b16_sb = p.sbuf("b16_sb", [16, 256], F32)
    nb16_sb = p.sbuf("nb16_sb", [16, 256], F32)
    brow_sb = p.sbuf("brow_sb", [1, S], F32)
    brb_sb = p.sbuf("brb_sb", [16, 128], F32)
    onw_sb = p.sbuf("onw_sb", [128, 512], F32)
    s_f32 = p.sbuf("s_f32", [128, 2, 512], F32)
    s_bf = p.sbuf("s_bf", [128, 2, 512], BF16)
    qn = p.sbuf("qn", [128, 2, 128], BF16)
    kn = p.sbuf("kn", [128, 2, 128], BF16)
    ktm16 = p.sbuf("ktm16", [16, 2, 128], BF16)
    sqsc = p.sbuf("sqsc", [128, 2, 128], BF16)
    rowsc = p.sbuf("rowsc", [1, 8 * 128], F32)
    n_pad = p.sbuf("n_pad", [16, 128], BF16)
    nt_pad = p.sbuf("nt_pad", [16, 128], BF16)
    np2_pad = p.sbuf("np2_pad", [16, 128], BF16)
    npt2_pad = p.sbuf("npt2_pad", [16, 128], BF16)
    np4_pad = p.sbuf("np4_pad", [16, 128], BF16)
    mt_pad = p.sbuf("mt_pad", [16, 128], BF16)
    tmp16 = p.sbuf("tmp16", [16, 128], F32)
    u_sb = p.sbuf("u_sb", [16, 512], BF16)
    qk_sb = p.sbuf("qk_sb", [16, 16], BF16)
    bv_sb = p.sbuf("bv_sb", [16, 512], F32)
    bm_sb = p.sbuf("bm_sb", [16, 512], BF16)
    on_sb = p.sbuf("on_sb", [16, 512], BF16)
    osq = p.sbuf("osq", [16, 512], BF16)
    colsc = p.sbuf("colsc", [16, 8], F32)

    d1 = p.dma("sync", onw_sb, onw)
    p.wait("vector", d1)
    for ap in (n_pad, nt_pad, np2_pad, npt2_pad, np4_pad, mt_pad):
        p.run("vector", lambda e, _a=ap: e.memset(_a, 0.0), mark=True)

    for u in range(NU):
        du1 = p.dma("sync", q_sb, QPh[u].rearrange("(m p) t -> p m t", p=128))
        du2 = p.dma("sync", k_sb, KPh[u].rearrange("(m p) t -> p m t", p=128))
        du4 = p.dma("sync", b16_sb, BETAc[u])
        du5 = p.dma("sync", brow_sb, BETAr[u])
        p.run("vector", lambda e: e.memset(s_f32[:, 0, :], 0.0))
        p.run("vector", lambda e: e.memset(s_f32[:, 1, :], 0.0))
        p.run("vector", lambda e: e.memset(s_bf[:, 0, :], 0.0))
        vdep = p.run("vector", lambda e: e.memset(s_bf[:, 1, :], 0.0), mark=True)
        p.wait("vector", du4)
        nbd = p.run("vector", lambda e: e.tensor_scalar_mul(nb16_sb, b16_sb, -1.0), mark=True)
        p.wait("tensor", [du1, du2, du5, vdep])
        p.wait("scalar", [du2, du4])
        for blk in range(32):
            t0 = blk * 128
            dv = p.dma("sync", vt_sb, VT16[u][:, blk * 8:(blk + 1) * 8, :])
            for which, src_t, dst_t, qscale in ((0, q_sb, qn, True), (1, k_sb, kn, False)):
                sdep = None
                for mh in range(2):
                    sdep = p.run("scalar", lambda e, _m=mh, _s=src_t, _t0=t0: e.activation(
                        sqsc[:, _m, :], _s[:, _m, _t0:_t0 + 128], AF.Square), mark=True)
                p.wait("tensor", sdep)
                iq, psq = p.psum_acquire()
                mdep = emit_matmul_group(p, psq, [(ones_bf, sqsc[:, mh, :], 1, 128) for mh in range(2)])
                rw = rowsc[:, which * 128:(which + 1) * 128]
                rw2 = rowsc[:, (2 + which) * 128:(3 + which) * 128]
                p.wait("scalar", mdep)
                s1 = p.run("scalar", lambda e, _r=rw, _ps=psq: e.activation(_r, _ps[0:1, 0:128], AF.Ln, bias=1e-6), mark=True)
                p.psum_release(iq, s1)
                p.wait("vector", s1)
                v1 = p.run("vector", lambda e, _r=rw, _r2=rw2: e.tensor_scalar_mul(_r2, _r, -0.5), mark=True)
                p.wait("scalar", v1)
                if qscale:
                    s2 = p.run("scalar", lambda e, _r2=rw2: e.activation(_r2, _r2, AF.Exp, bias=-2.7725887222397811), mark=True)
                else:
                    s2 = p.run("scalar", lambda e, _r2=rw2: e.activation(_r2, _r2, AF.Exp), mark=True)
                ib, psb = p.psum_acquire()
                p.wait("tensor", s2)
                bdep = emit_matmul_group(p, psb, [(ones_f, rw2, 128, 128)])
                p.wait("vector", bdep)
                vdep = None
                for mh in range(2):
                    vdep = p.run("vector", lambda e, _m=mh, _s=src_t, _d=dst_t, _t0=t0, _ps=psb: e.tensor_tensor(
                        _d[:, _m, :], _s[:, _m, _t0:_t0 + 128], _ps[:, 0:128], ALU.mult), mark=True)
                p.psum_release(ib, vdep)
            ibr, psbr = p.psum_acquire()
            p.wait("tensor", [vdep, nbd])
            brd = emit_matmul_group(p, psbr, [(ones_f[0:1, 0:16], brow_sb[:, t0:t0 + 128], 16, 128)])
            p.wait("vector", brd)
            brc = p.run("vector", lambda e, _ps=psbr: e.tensor_tensor(
                brb_sb, _ps[0:16, 0:128], utsT_tiled, ALU.mult), mark=True)
            p.psum_release(ibr, brc)
            p.wait("tensor", dv)
            for sc_ in range(8):
                sub = blk * 8 + sc_
                sl = slice(sc_ * 16, sc_ * 16 + 16)
                bcol = b16_sb[:, sub:sub + 1]
                nbcol = nb16_sb[:, sub:sub + 1]
                ig, psg = p.psum_acquire()
                gdep = emit_matmul_group(p, psg, [(kn[:, mh, sl], kn[:, mh, :], 16, 128) for mh in range(2)])
                p.wait("vector", gdep)
                p.run("vector", lambda e, _ps=psg, _sl=sl, _b=nbcol: e.scalar_tensor_tensor(
                    n_pad[:, 0:16], _ps[0:16, _sl], _b, ltm, ALU.mult, ALU.mult))
                ndep = p.run("vector", lambda e, _ps=psg, _sl=sl: e.tensor_tensor(
                    nt_pad[:, 0:16], _ps[0:16, _sl], brb_sb[:, _sl], ALU.mult), mark=True)
                p.psum_release(ig, ndep)
                ik2, psk2 = p.psum_acquire()
                kqd = emit_matmul_group(p, psk2, [(kn[:, mh, sl], qn[:, mh, :], 16, 128) for mh in range(2)])
                p.wait("vector", kqd)
                qkd = p.run("vector", lambda e, _ps=psk2, _sl=sl: e.tensor_tensor(
                    qk_sb, _ps[0:16, _sl], utim, ALU.mult), mark=True)
                p.psum_release(ik2, qkd)
                p.wait("vector", ndep)
                mtd = p.run("vector", lambda e: e.tensor_add(mt_pad, ident16, nt_pad), mark=True)
                p.wait("tensor", [mtd, ndep])
                i1_, ps1 = p.psum_acquire()
                d_np2 = emit_matmul_group(p, ps1, [(nt_pad[:, 0:16], n_pad, 16, 128)])
                i2_, ps2 = p.psum_acquire()
                d_npt2 = emit_matmul_group(p, ps2, [(n_pad[:, 0:16], nt_pad, 16, 128)])
                p.wait("scalar", d_np2)
                c1 = p.run("scalar", lambda e, _ps=ps1: e.copy(np2_pad, _ps[0:16, 0:128]), mark=True)
                p.wait("scalar", d_npt2)
                c2 = p.run("scalar", lambda e, _ps=ps2: e.copy(npt2_pad, _ps[0:16, 0:128]), mark=True)
                p.psum_release(i1_, c1)
                p.psum_release(i2_, c2)
                p.wait("tensor", [c1, c2])
                i3_, ps3 = p.psum_acquire()
                d_np4 = emit_matmul_group(p, ps3, [(npt2_pad[:, 0:16], np2_pad, 16, 128)])
                p.wait("scalar", d_np4)
                c3 = p.run("scalar", lambda e, _ps=ps3: e.copy(np4_pad, _ps[0:16, 0:128]), mark=True)
                p.psum_release(i3_, c3)
                i4_, ps4 = p.psum_acquire()
                d_m2 = emit_matmul_group(p, ps4, [(np2_pad[:, 0:16], mt_pad, 16, 128)])
                p.wait("vector", d_m2)
                mtd = p.run("vector", lambda e, _ps=ps4: e.tensor_add(mt_pad, mt_pad, _ps[0:16, 0:128]), mark=True)
                p.psum_release(i4_, mtd)
                p.wait("tensor", [mtd, c3])
                i5_, ps5 = p.psum_acquire()
                d_m4 = emit_matmul_group(p, ps5, [(np4_pad[:, 0:16], mt_pad, 16, 128)])
                p.wait("vector", d_m4)
                mtd = p.run("vector", lambda e, _ps=ps5: e.tensor_add(mt_pad, mt_pad, _ps[0:16, 0:128]), mark=True)
                p.psum_release(i5_, mtd)
                iks, psks = p.psum_acquire()
                ksd = emit_matmul_group(p, psks, [(kn[:, mh, sl], s_bf[:, mh, :], 16, 512) for mh in range(2)])
                p.wait("vector", ksd)
                p.run("vector", lambda e, _s=sc_, _b=bcol: e.tensor_scalar(
                    bv_sb, vt_sb[:, _s, :], _b, None, ALU.mult))
                bmd = p.run("vector", lambda e, _ps=psks, _b=nbcol: e.scalar_tensor_tensor(
                    bm_sb, _ps[0:16, :], _b, bv_sb, ALU.mult, ALU.add), mark=True)
                p.psum_release(iks, bmd)
                p.wait("tensor", [bmd, mtd])
                iu, psu = p.psum_acquire()
                ud = emit_matmul_group(p, psu, [(mt_pad[:, 0:16], bm_sb, 16, 512)])
                p.wait("scalar", ud)
                ucd = p.run("scalar", lambda e, _ps=psu: e.copy(u_sb, _ps[0:16, :]), mark=True)
                p.psum_release(iu, ucd)
                p.wait("tensor", [qkd, ucd])
                io, pso = p.psum_acquire()
                odep = emit_matmul_group(p, pso, [(qn[:, 0, sl], s_bf[:, 0, :], 16, 512),
                                                  (qn[:, 1, sl], s_bf[:, 1, :], 16, 512),
                                                  (qk_sb, u_sb, 16, 512)])
                p.wait("scalar", odep)
                sd1 = p.run("scalar", lambda e, _ps=pso: e.activation(osq, _ps[0:16, :], AF.Square, accum_out=colsc[:, 1:2]), mark=True)
                p.wait("vector", sd1)
                vd1 = p.run("vector", lambda e: e.tensor_scalar_mul(colsc[:, 2:3], colsc[:, 1:2], 1.0 / 512.0), mark=True)
                p.wait("scalar", vd1)
                sd2 = p.run("scalar", lambda e: e.activation(colsc[:, 3:4], colsc[:, 2:3], AF.Ln, bias=1e-5), mark=True)
                p.wait("vector", sd2)
                vd2 = p.run("vector", lambda e: e.tensor_scalar_mul(colsc[:, 4:5], colsc[:, 3:4], -0.5), mark=True)
                p.wait("scalar", vd2)
                sd3 = p.run("scalar", lambda e: e.activation(colsc[:, 5:6], colsc[:, 4:5], AF.Exp), mark=True)
                p.wait("vector", sd3)
                ond = p.run("vector", lambda e, _ps=pso: e.scalar_tensor_tensor(
                    on_sb, _ps[0:16, :], colsc[:, 5:6], onw_sb[0:16, :], ALU.mult, ALU.mult), mark=True)
                p.wait("sync", ond)
                std = p.dma("sync", ON[u][sub * 16:sub * 16 + 16, :], on_sb, "os")
                if u == 0 and blk == 0 and sc_ == 0:
                    p.wait("gpsimd", [ond, mtd, ucd, qkd, bmd])
                    p.dma("gpsimd", DBG[:, 0:128], mt_pad, "os")
                    p.dma("gpsimd", DBG[:, 128:640], u_sb, "os")
                    p.dma("gpsimd", DBG[:, 640:1152], bm_sb, "os")
                    p.dma("gpsimd", DBG[:, 1152:1168], qk_sb, "os")
                    p.dma("gpsimd", DBG[:, 1168:1296], n_pad, "os")
                    p.dma("gpsimd", DBG[:, 1296:1424], nt_pad, "os")
                    p.dma("gpsimd", DBG[:, 1424:1680], nb16_sb, "os")
                    p.dma("gpsimd", DBG[:, 1680:1808], brb_sb, "os")
                ktd = []
                for mh in range(2):
                    td = p.run("tensor", lambda e, _m=mh, _sl=sl: e.transpose(
                        psT[0:16, _m * 128:(_m + 1) * 128], kn[:, _m, _sl], ident), mark=True)
                    p.wait("vector", td)
                    cd = p.run("vector", lambda e, _m=mh: e.tensor_copy(
                        ktm16[:, _m, :], psT[0:16, _m * 128:(_m + 1) * 128]), mark=True)
                    ktd.append(cd)
                p.wait("tensor", ktd[-1])
                sud = None
                for mh in range(2):
                    isu, pssu = p.psum_acquire()
                    sd = emit_matmul_group(p, pssu, [(ktm16[:, mh, :], u_sb, 128, 512)])
                    p.wait("vector", sd)
                    sud = p.run("vector", lambda e, _m=mh, _ps=pssu: e.tensor_add(
                        s_f32[:, _m, :], s_f32[:, _m, :], _ps), mark=True)
                    sbd = p.run("vector", lambda e, _m=mh: e.tensor_copy(
                        s_bf[:, _m, :], s_f32[:, _m, :]), mark=True)
                    p.psum_release(isu, sud)
                p.psum_release(io, ond)
                chunk_barrier(p)
        chunk_barrier(p)
    return p


def prep_l2attn(ins, o5, c):
    qs, ks, vs_, bs, brs_ = [], [], [], [], []
    for uu in (2 * c, 2 * c + 1):
        b, h = uu // 4, uu % 4
        qp = np.concatenate([o5[2 * b]["QP"], o5[2 * b + 1]["QP"]], axis=1)
        kp = np.concatenate([o5[2 * b]["KP"], o5[2 * b + 1]["KP"]], axis=1)
        vp = np.concatenate([o5[2 * b]["VP"], o5[2 * b + 1]["VP"]], axis=1)
        bt = np.concatenate([o5[2 * b]["BETA"], o5[2 * b + 1]["BETA"]], axis=1)
        qs.append(qp[h * 256:(h + 1) * 256])
        ks.append(kp[h * 256:(h + 1) * 256])
        vt = np.ascontiguousarray(vp[h * 512:(h + 1) * 512].T)
        vs_.append(np.ascontiguousarray(vt.reshape(256, 16, 512).transpose(1, 0, 2)))
        bs.append(np.ascontiguousarray(bt[h].astype(np.float32).reshape(256, 16).T))
        brs_.append(np.ascontiguousarray(bt[h].astype(np.float32)[None, :]))
    return {
        "QPh": np.stack(qs), "KPh": np.stack(ks), "VT16": np.stack(vs_),
        "BETAc": np.stack(bs), "BETAr": np.stack(brs_),
        "onw": _f32(np.tile(np.asarray(ins["L2_attn_o_norm_w"], np.float32)[None, :], (128, 1))),
    }


def build_l3pre(p=None, io=None):
    p = p or Prog("l3pre")
    io = io or ExtIO(p)
    X2i = io.inp("X2i", [D, TC])
    XN2Oi = io.inp("XN2Oi", [D, TC])
    ONf = io.inp("ONf", [2048, TC], BF16)
    FFN2i = io.inp("FFN2i", [D, TC])
    Wo = io.inp("Wo", [2048, D])
    W_in3 = io.inp("W_in3", [D, 32])
    b_in3 = io.inp("b_in3", [32, 1])
    W_dt1 = io.inp("W_dt1", [D, 64])
    b_dt1 = io.inp("b_dt1", [64, 1])
    W_dt2 = io.inp("W_dt2", [64, D])
    b_dt2 = io.inp("b_dt2", [128, 8])
    inprojT = io.inp("inprojT", [D, 3072])
    inprojb = io.inp("inprojb", [128, 24])

    X3 = io.out("X3", [D, TC])
    XN3 = io.out("XN3", [D, TC])
    BC3 = io.out("BC3", [32, TC])
    DT3 = io.out("DT3", [D, TC])
    DTX = io.out("DTX", [D, TC])
    QKV3 = io.out("QKV3", [3072, TC], BF16)

    p.regconsts(EPS, 1.0, -1.0, -0.5, 1e-7, 1e6)
    ones_bf, ob_dep = p.const_sbuf(np.ones((128, 1), dtype=BF))
    ones_f, of_dep = p.const_sbuf(np.ones((1, 128), dtype=np.float32))
    p.wait("tensor", [ob_dep, of_dep])
    p.psum_pool(8)
    r = lambda ap: ap.rearrange("(m p) t -> p m t", p=128)

    wo_sb = p.sbuf("wo_sb", [128, 16, 1024], BF16)
    win_sb = p.sbuf("win_sb", [128, 8, 32], BF16)
    bin_sb = p.sbuf("bin_sb", [32, 1], F32)
    wdt1_sb = p.sbuf("wdt1_sb", [128, 8, 64], BF16)
    bdt1_sb = p.sbuf("bdt1_sb", [64, 1], F32)
    wdt2_sb = p.sbuf("wdt2_sb", [64, 1024], BF16)
    bdt2_sb = p.sbuf("bdt2_sb", [128, 8], F32)
    ipj_sb = p.sbuf("ipj_sb", [128, 8, 3072], BF16)
    ipb_sb = p.sbuf("ipb_sb", [128, 24], F32)
    x2c = p.sbuf("x2c", [128, 8, 512], F32)
    xnoc = p.sbuf("xnoc", [128, 8, 512], F32)
    onc = p.sbuf("onc", [128, 16, 512], BF16)
    f2c = p.sbuf("f2c", [128, 8, 512], F32)
    x3c = p.sbuf("x3c", [128, 8, 512], F32)
    x3b = p.sbuf("x3b", [128, 8, 512], BF16)
    xn3c = p.sbuf("xn3c", [128, 8, 512], F32)
    xn3b = x3b
    dt1b = p.sbuf("dt1b", [64, 512], BF16)
    dtc = f2c
    spt = p.sbuf("spt", [128, 512], F32)
    sq2 = [p.sbuf("sq2a", [128, 512], BF16), p.sbuf("sq2b", [128, 512], BF16)]
    stat = p.sbuf("stat", [1, 2048], F32)
    t1s = [p.sbuf("t1a", [128, 512], F32), p.sbuf("t1b", [128, 512], F32)]

    wdeps = []
    for k in range(16):
        wdeps.append(p.dma("gpsimd", wo_sb[:, k, :], Wo[k * 128:(k + 1) * 128, :]))
    for k in range(8):
        wdeps.append(p.dma("gpsimd", win_sb[:, k, :], W_in3[k * 128:(k + 1) * 128, :]))
        wdeps.append(p.dma("gpsimd", wdt1_sb[:, k, :], W_dt1[k * 128:(k + 1) * 128, :]))
        wdeps.append(p.dma("gpsimd", ipj_sb[:, k, :], inprojT[k * 128:(k + 1) * 128, :]))
    wdeps.append(p.dma("gpsimd", wdt2_sb, W_dt2))
    for dst, src_ in ((bin_sb, b_in3), (bdt1_sb, b_dt1), (bdt2_sb, b_dt2), (ipb_sb, inprojb)):
        wdeps.append(p.dma("sync", dst, src_))
    p.wait("tensor", [wdeps[-5], wdeps[-1]])
    p.wait("scalar", wdeps[-1])
    p.wait("vector", wdeps[-1])

    lnctx = {}
    for ch in range(4):
        c0 = ch * 512
        p.dma("sync", x2c, r(X2i)[:, :, c0:c0 + 512])
        p.dma("sync", xnoc, r(XN2Oi)[:, :, c0:c0 + 512])
        p.dma("sync", onc, r(ONf)[:, :, c0:c0 + 512])
        ld = p.dma("sync", f2c, r(FFN2i)[:, :, c0:c0 + 512])
        p.wait("tensor", ld)
        p.wait("vector", ld)

        odeps = []
        def evacO(m, chx, ps, dep):
            p.wait("vector", dep)
            p.run("vector", lambda e, _m=m: e.tensor_add(x3c[:, _m, :], x2c[:, _m, :], ps))
            p.run("vector", lambda e, _m=m: e.tensor_add(x3c[:, _m, :], x3c[:, _m, :], xnoc[:, _m, :]))
            p.run("vector", lambda e, _m=m: e.tensor_add(x3c[:, _m, :], x3c[:, _m, :], f2c[:, _m, :]))
            dd = p.run("vector", lambda e, _m=m: e.tensor_copy(x3b[:, _m, :], x3c[:, _m, :]), mark=True)
            odeps.append(dd)
            return dd
        emit_linear(p, nk=16, nm=8, chunks=[(0, 512)],
                    wf=lambda k, m: wo_sb[:, k, m * 128:(m + 1) * 128],
                    xf=lambda k, _c: onc[:, k, :], evac=evacO)
        vS = p.sem("vs")
        p.wait("sync", (vS, vS.n))
        p.dma("sync", r(X3)[:, :, c0:c0 + 512], x3c, "os")

        def out_cb(k, bnm, brs):
            t1 = t1s[k % 2]
            p.run("vector", lambda e, _k=k: e.tensor_add(t1, x3c[:, _k, :], bnm))
            vdep = p.run("vector", lambda e, _k=k, _t=t1: e.tensor_tensor(xn3c[:, _k, :], _t, brs, ALU.mult), mark=True)
            vdep2 = p.run("vector", lambda e, _k=k: e.tensor_copy(xn3b[:, _k, :], xn3c[:, _k, :]), mark=True)
            return vdep2
        lndep = emit_ln(p, lnctx, xbf=lambda k: x3b[:, k, :], nk=8, ncol=512,
                        ones_bf=ones_bf, ones_f=ones_f, sq2=sq2, stat=stat, out_cb=out_cb,
                        xdep=odeps[-1])
        p.wait("sync", lndep)
        p.dma("sync", r(XN3)[:, :, c0:c0 + 512], xn3c, "os")
        p.wait("tensor", lndep)

        stdeps = []
        def evacB(m, chx, ps, dep, _c0=c0):
            p.wait("scalar", dep)
            sdep = p.run("scalar", lambda e: e.activation(spt[0:32, :], ps[0:32, :], AF.Identity, bias=bin_sb), mark=True)
            p.wait("sync", sdep)
            stdeps.append(p.dma("sync", BC3[:, _c0:_c0 + 512], spt[0:32, :], "os"))
            return sdep
        emit_linear(p, nk=8, nm=1, chunks=[(0, 512)],
                    wf=lambda k, m: win_sb[:, k, :], xf=lambda k, _c: xn3b[:, k, :], evac=evacB)
        def evacD1(m, chx, ps, dep):
            p.wait("scalar", dep)
            return p.run("scalar", lambda e: e.activation(dt1b, ps[0:64, :], AF.Identity, bias=bdt1_sb), mark=True)
        emit_linear(p, nk=8, nm=1, chunks=[(0, 512)],
                    wf=lambda k, m: wdt1_sb[:, k, :], xf=lambda k, _c: xn3b[:, k, :], evac=evacD1)
        sS = p.sem("ss")
        p.wait("tensor", (sS, sS.n))
        dstate = []
        def evacD2(m, chx, ps, dep, _c0=c0):
            p.wait("scalar", dep)
            p.wait("scalar", stdeps[-1])
            if dstate:
                p.wait("scalar", dstate[-1])
            p.run("scalar", lambda e, _m=m: e.activation(spt, ps, AF.Abs, bias=bdt2_sb[:, _m:_m + 1]))
            p.run("scalar", lambda e: e.activation(spt, spt, AF.Exp, scale=-1.0))
            p.run("scalar", lambda e: e.activation(spt, spt, AF.Ln, bias=1.0))
            sdep = p.run("scalar", lambda e, _m=m: e.activation(dtc[:, _m, :], ps, AF.Relu, bias=bdt2_sb[:, _m:_m + 1]), mark=True)
            p.wait("vector", sdep)
            p.run("vector", lambda e, _m=m: e.tensor_add(dtc[:, _m, :], dtc[:, _m, :], spt))
            vdep = p.run("vector", lambda e, _m=m: e.tensor_scalar(
                dtc[:, _m, :], dtc[:, _m, :], 1e-7, 1e6, ALU.max, ALU.min), mark=True)
            dstate.append(vdep)
            return vdep
        emit_linear(p, nk=1, nm=8, chunks=[(0, 512)],
                    wf=lambda k, m: wdt2_sb[:, m * 128:(m + 1) * 128],
                    xf=lambda k, _c: dt1b, evac=evacD2)
        vS = p.sem("vs")
        p.wait("sync", (vS, vS.n))
        dtst = p.dma("sync", r(DT3)[:, :, c0:c0 + 512], dtc, "os")
        p.wait("vector", dtst)
        vdep = None
        for m in range(8):
            vdep = p.run("vector", lambda e, _m=m: e.tensor_mul(dtc[:, _m, :], dtc[:, _m, :], xn3c[:, _m, :]), mark=True)
        p.wait("sync", vdep)
        p.dma("sync", r(DTX)[:, :, c0:c0 + 512], dtc, "os")
        def evacQ(m, chx, ps, dep, _c0=c0):
            p.wait("scalar", dep)
            p.wait("scalar", stdeps[-1])
            p.wait("scalar", dstate[-1])
            sdep = p.run("scalar", lambda e, _m=m: e.activation(
                spt.bitcast(BF16)[:, 0:512], ps, AF.Identity, bias=ipb_sb[:, _m:_m + 1]), mark=True)
            p.wait("sync", sdep)
            stdeps.append(p.dma("sync", QKV3[m * 128:(m + 1) * 128, _c0:_c0 + 512], spt.bitcast(BF16)[:, 0:512], "os"))
            return sdep
        emit_linear(p, nk=8, nm=24, chunks=[(0, 512)],
                    wf=lambda k, m: ipj_sb[:, k, m * 128:(m + 1) * 128],
                    xf=lambda k, _c: xn3b[:, k, :], evac=evacQ)
        chunk_barrier(p)
    return p


def prep_l3pre(ins, o5, o6, c):
    b, hf = c // 2, c % 2
    t0 = hf * TC
    on_rows = []
    for h in range(4):
        uu = b * 4 + h
        cc, ui = uu // 2, uu % 2
        on_rows.append(np.ascontiguousarray(o6[cc]["ON"][ui][t0:t0 + TC].T))
    onf = np.concatenate(on_rows, axis=0)
    return {
        "X2i": o5[c]["X2"], "XN2Oi": o5[c]["XN2O"], "ONf": np.ascontiguousarray(onf),
        "FFN2i": o5[c]["FFN2"],
        "Wo": _f32(ins["L2_attn_Wo"]),
        "W_in3": _f32(ins["L3_attn_W_in"]), "b_in3": _f32(np.asarray(ins["L3_attn_b_in"])[:, None]),
        "W_dt1": _f32(ins["L3_attn_W_dt1"]), "b_dt1": _f32(np.asarray(ins["L3_attn_b_dt1"])[:, None]),
        "W_dt2": _f32(ins["L3_attn_W_dt2"]), "b_dt2": col_tiles(ins["L3_attn_b_dt2"]),
        "inprojT": _f32(np.asarray(ins["L3_ffn_in_proj_w"]).T), "inprojb": col_tiles(ins["L3_ffn_in_proj_b"]),
    }


def build_l3scan(p=None, io=None):
    p = p or Prog("l3scan")
    io = io or ExtIO(p)
    TG, TH = 4096, 2048
    dt_s = io.inp("dt_s", [512, TG])
    dtx_s = io.inp("dtx_s", [512, TG])
    acol_in = io.inp("acol_in", [128, 64])
    Brow = io.inp("Brow", [16, TG])
    Crow = io.inp("Crow", [16, TG])
    Y = io.out("Y", [512, TG])

    blk_ = np.zeros((128, 8), dtype=BF)
    for q in range(128):
        blk_[q, q // 16] = 1
    p.regconsts(1.0, -1.0)
    p.psum_pool(8)
    acol = p.sbuf("acol", [128, 64], F32)
    ainv = p.sbuf("ainv", [128, 64], F32)
    nainv = p.sbuf("nainv", [128, 64], F32)
    dt_sb = p.sbuf("dt_sb", [128, 4, TH], F32)
    dtx_sb = p.sbuf("dtx_sb", [128, 4, TH], F32)
    y_sb = p.sbuf("y_sb", [128, 4, TH], F32)
    bbc = p.sbuf("bbc", [128, TH], F32)
    cbc = p.sbuf("cbc", [128, TH], F32)
    dA = [p.sbuf("dA0", [128, TH], F32), p.sbuf("dA1", [128, TH], F32)]
    fb = p.sbuf("fb", [128, TH], F32)
    dBx = p.sbuf("dBx", [128, TH], F32)
    hbuf = p.sbuf("hbuf", [128, TH], F32)
    carry = p.sbuf("carry", [128, 64], F32)

    d1 = p.dma("sync", acol, acol_in)
    p.wait("vector", d1)
    vd = p.run("vector", lambda e: e.reciprocal(ainv, acol), mark=True)
    p.wait("scalar", vd)
    sd = p.run("scalar", lambda e: e.mul(nainv, ainv, -1.0), mark=True)
    p.wait("scalar", d1)
    p.barrier()

    dt_r = dt_s.rearrange("(m p) t -> p m t", p=128)
    dtx_r = dtx_s.rearrange("(m p) t -> p m t", p=128)
    Y_r = Y.rearrange("(m p) t -> p m t", p=128)
    for tch in range(2):
        c0 = tch * TH
        p.dma("sync", dt_sb, dt_r[:, :, c0:c0 + TH])
        ldd = p.dma("sync", dtx_sb, dtx_r[:, :, c0:c0 + TH])
        p.wait("scalar", ldd)
        p.wait("vector", ldd)
        for n in range(16):
            p.dma("sync", bbc, Brow[n:n + 1, c0:c0 + TH].broadcast_to([128, TH]))
            bd2 = p.dma("sync", cbc, Crow[n:n + 1, c0:c0 + TH].broadcast_to([128, TH]))
            p.wait("vector", bd2)
            for dtile in range(4):
                ci = dtile * 16 + n
                pp = (n * 4 + dtile) % 2
                sdep = p.run("scalar", lambda e, _d=dtile, _c=ci, _p=pp: e.activation(
                    dA[_p], dt_sb[:, _d, :], AF.Exp, scale=acol[:, _c:_c + 1]), mark=True)
                p.wait("vector", sdep)
                p.run("vector", lambda e, _p=pp, _c=ci: e.tensor_scalar(
                    fb, dA[_p], ainv[:, _c:_c + 1], nainv[:, _c:_c + 1], ALU.mult, ALU.add))
                p.run("vector", lambda e, _d=dtile: e.tensor_mul(dBx, dtx_sb[:, _d, :], bbc))
                p.run("vector", lambda e: e.tensor_mul(dBx, dBx, fb))
                init = 0.0 if tch == 0 else carry[:, ci:ci + 1]
                vdep = p.run("vector", lambda e, _p=pp, _i=init: e.tensor_tensor_scan(
                    hbuf, dA[_p], dBx, _i, ALU.mult, ALU.add), mark=True)
                p.run("vector", lambda e: e.tensor_mul(dBx, hbuf, cbc))
                if tch == 0:
                    p.run("vector", lambda e, _c=ci: e.tensor_copy(carry[:, _c:_c + 1], hbuf[:, TH - 1:TH]))
                if n == 0:
                    vdep = p.run("vector", lambda e, _d=dtile: e.tensor_copy(y_sb[:, _d, :], dBx), mark=True)
                else:
                    vdep = p.run("vector", lambda e, _d=dtile: e.tensor_add(y_sb[:, _d, :], y_sb[:, _d, :], dBx), mark=True)
                p.wait("scalar", vdep)
            p.wait("sync", vdep)
        p.dma("sync", Y_r[:, :, c0:c0 + TH], y_sb, "os")
        chunk_barrier(p)
    return p


def prep_l3scan(ins, o7, c):
    b, half = c // 2, c % 2
    d0 = half * 512
    dt_b = np.concatenate([o7[2 * b]["DT3"], o7[2 * b + 1]["DT3"]], axis=1)
    dtx_b = np.concatenate([o7[2 * b]["DTX"], o7[2 * b + 1]["DTX"]], axis=1)
    bc_b = np.concatenate([o7[2 * b]["BC3"], o7[2 * b + 1]["BC3"]], axis=1)
    A = np.asarray(ins["L3_attn_A"], np.float32)[0, 0]
    asl = A[d0:d0 + 512]
    al = np.concatenate([asl[i * 128:(i + 1) * 128, :] for i in range(4)], axis=1)
    return {
        "dt_s": _f32(dt_b[d0:d0 + 512]), "dtx_s": _f32(dtx_b[d0:d0 + 512]),
        "acol_in": _f32(al),
        "Brow": _f32(bc_b[0:16]), "Crow": _f32(bc_b[16:32]),
    }


def build_l3ffnA(p=None, io=None):
    p = p or Prog("l3ffnA")
    io = io or ExtIO(p)
    QKVtm = io.inp("QKVtm", [4, 512, 3072], BF16)
    OA = io.out("OA", [4, 512, 1024], BF16)

    p.regconsts(1.0, -1.0, 0.0625)
    p.psum_pool(8)
    qkv = p.sbuf("qkv", [128, 4, 3072], BF16)
    scr = p.sbuf("scr", [128, 64], F32)
    tmp256 = p.sbuf("tmp256", [128, 256], F32)
    rmax16 = p.sbuf("rmax16", [128, 16], F32)
    rneg16 = p.sbuf("rneg16", [128, 16], F32)
    exps = p.sbuf("exps", [128, 64], F32)
    sums16 = p.sbuf("sums16", [128, 16], F32)
    sums2 = p.sbuf("sums2", [128, 16], F32)
    rec16 = p.sbuf("rec16", [128, 16], F32)
    rec2 = p.sbuf("rec2", [128, 16], F32)
    attn = p.sbuf("attn", [128, 64], F32)
    attn2 = p.sbuf("attn2", [128, 64], F32)
    oacc = p.sbuf("oacc", [128, 16, 256], F32)

    for tile in range(4):
        ld = p.dma("sync", qkv, QKVtm[:, tile * 128:(tile + 1) * 128, :].rearrange("l p d -> p l d"))
        p.wait("vector", ld)
        vdep = None
        for h in range(4):
            for l in range(4):
                for m in range(4):
                    col = h * 16 + l * 4 + m
                    p.run("vector", lambda e, _h=h, _l=l, _m=m: e.tensor_mul(
                        tmp256, qkv[:, _l, _h * 256:(_h + 1) * 256],
                        qkv[:, _m, 1024 + _h * 256:1024 + (_h + 1) * 256]))
                    vdep = p.run("vector", lambda e, _c=col: e.reduce_sum(
                        scr[:, _c:_c + 1], tmp256, axis=AX.X), mark=True)
        v1 = p.run("vector", lambda e: e.reduce_max(
            rmax16, scr.rearrange("p (g m) -> p g m", m=4), axis=AX.X), mark=True)
        p.wait("gpsimd", v1)
        g1 = p.run("gpsimd", lambda e: e.tensor_scalar_mul(rneg16, rmax16, -1.0), mark=True)
        p.wait("vector", g1)
        v2 = p.run("vector", lambda e: e.tensor_tensor(
            scr.rearrange("p (g m) -> p g m", m=4), scr.rearrange("p (g m) -> p g m", m=4),
            rneg16.unsqueeze(2).broadcast_to([128, 16, 4]), ALU.add), mark=True)
        p.wait("scalar", v2)
        s1 = p.run("scalar", lambda e: e.activation(exps, scr, AF.Exp, scale=0.0625), mark=True)
        p.wait("vector", s1)
        v3 = p.run("vector", lambda e: e.reduce_sum(
            sums16, exps.rearrange("p (g m) -> p g m", m=4), axis=AX.X), mark=True)
        p.wait("gpsimd", v3)
        g2 = p.run("gpsimd", lambda e: e.tensor_copy(sums2, sums16), mark=True)
        p.wait("vector", g2)
        v4 = p.run("vector", lambda e: e.reciprocal(rec16, sums2), mark=True)
        p.wait("gpsimd", v4)
        g3 = p.run("gpsimd", lambda e: e.tensor_copy(rec2, rec16), mark=True)
        p.wait("vector", g3)
        v5 = p.run("vector", lambda e: e.tensor_tensor(
            attn.rearrange("p (g m) -> p g m", m=4), exps.rearrange("p (g m) -> p g m", m=4),
            rec2.unsqueeze(2).broadcast_to([128, 16, 4]), ALU.mult), mark=True)
        p.wait("gpsimd", v5)
        g4 = p.run("gpsimd", lambda e: e.tensor_copy(attn2, attn), mark=True)
        p.wait("vector", g4)
        odep = None
        for h in range(4):
            for l in range(4):
                gi = h * 4 + l
                for m in range(4):
                    col = h * 16 + l * 4 + m
                    if m == 0:
                        p.run("vector", lambda e, _g=gi, _c=col, _m=m, _h=h: e.tensor_scalar(
                            oacc[:, _g, :], qkv[:, _m, 2048 + _h * 256:2048 + (_h + 1) * 256],
                            attn2[:, _c:_c + 1], None, ALU.mult))
                    else:
                        p.run("vector", lambda e, _g=gi, _c=col, _m=m, _h=h: e.scalar_tensor_tensor(
                            oacc[:, _g, :], qkv[:, _m, 2048 + _h * 256:2048 + (_h + 1) * 256],
                            attn2[:, _c:_c + 1], oacc[:, _g, :], ALU.mult, ALU.add))
                odep = p.run("vector", lambda e: e.engine_nop(), mark=True)
        p.wait("gpsimd", odep)
        for h in range(4):
            for l in range(4):
                p.dma("gpsimd", OA[l][tile * 128:(tile + 1) * 128, h * 256:(h + 1) * 256],
                      oacc[:, h * 4 + l, :], "os")
        chunk_barrier(p)
    return p


def prep_l3ffnA(ins, o7, c):
    out = np.zeros((4, 512, 3072), dtype=BF)
    n0 = c * 512
    for l in range(4):
        tscore = 2 * l + (1 if n0 >= 2048 else 0)
        tloc = n0 - (2048 if n0 >= 2048 else 0)
        out[l] = o7[tscore]["QKV3"][:, tloc:tloc + 512].T
    return {"QKVtm": np.ascontiguousarray(out)}


def build_l3ffnB(p=None, io=None):
    p = p or Prog("l3ffnB")
    io = io or ExtIO(p)
    XN3i = io.inp("XN3i", [D, TC])
    OAf = io.inp("OAf", [D, TC], BF16)
    oprojT = io.inp("oprojT", [D, D])
    oprojb = io.inp("oprojb", [128, 8])
    ln1g = io.inp("ln1g", [128, 8])
    ln1b = io.inp("ln1b", [128, 8])
    wg3 = io.inp("wg3", [128, 8, 512])

    Y1 = io.out("Y1", [D, TC])
    Y1B = io.out("Y1B", [D, TC], BF16)
    MASKA = io.out("MASKA", [128, 128])
    LOCROW = io.out("LOCROW", [1, TC])
    CNTR = io.out("CNTR", [128, 128])

    p.regconsts(EPS, 1.0, -1.0, -0.5)
    ones_bf, ob_dep = p.const_sbuf(np.ones((128, 1), dtype=BF))
    ones_f, of_dep = p.const_sbuf(np.ones((1, 128), dtype=np.float32))
    ones8f, o8_dep = p.const_sbuf(np.ones((8, 1), dtype=np.float32))
    eye8p = np.zeros((8, 128), dtype=np.float32)
    eye8p[:, 0:8] = np.eye(8)
    eye8pad, e8_dep = p.const_sbuf(eye8p)
    ident, id_dep = p.const_sbuf(np.eye(128, dtype=BF))
    zeros16, z_dep = p.const_sbuf(np.zeros((16, 2048), dtype=np.float32))
    p.wait("tensor", [ob_dep, of_dep, o8_dep, id_dep, e8_dep])
    p.psum_pool(7)
    psT = p.psum_t("psT", [128, 1024], BF16)

    wp_sb = p.sbuf("wp_sb", [128, 8, 1024], BF16)
    wg_sb = p.sbuf("wg_sb", [128, 8, 512], BF16)
    bp_sb = p.sbuf("bp_sb", [128, 8], F32)
    g1_sb = p.sbuf("g1_sb", [128, 8], F32)
    b1_sb = p.sbuf("b1_sb", [128, 8], F32)
    xnc = p.sbuf("xnc", [128, 8, 512], F32)
    oac = p.sbuf("oac", [128, 8, 512], BF16)
    y1c = p.sbuf("y1c", [128, 8, 512], F32)
    y1bc = p.sbuf("y1bc", [128, 8, 512], BF16)
    y1bb = p.sbuf("y1bb", [128, 8, TC], BF16)
    sq2 = [p.sbuf("sq2a", [128, 512], BF16), p.sbuf("sq2b", [128, 512], BF16)]
    stat = p.sbuf("stat", [1, 2048], F32)
    t1s = [p.sbuf("t1a", [128, 512], F32), p.sbuf("t1b", [128, 512], F32)]
    g_f = p.sbuf("g_f", [128, 8], F32)
    ex_t = p.sbuf("ex_t", [128, 8], F32)
    prb = p.sbuf("prb", [128, 8], F32)
    m1c = p.sbuf("m1c", [128, 8], F32)
    csc = p.sbuf("csc", [128, 8], F32)
    e1c = p.sbuf("e1c", [128, 8], F32)
    zeros8 = p.sbuf("zeros8", [128, 8], F32)
    scol = p.sbuf("scol", [128, 8], F32)
    mask_tm = p.sbuf("mask_tm", [128, 16, 8], F32)
    fpad = p.sbuf("fpad", [128, 16], BF16)
    ohfm = p.sbuf("ohfm", [16, TC], F32)
    cumf = p.sbuf("cumf", [16, TC], F32)
    tmp8 = p.sbuf("tmp8", [8, TC], F32)
    locsb = p.sbuf("locsb", [1, TC], F32)

    wdeps = []
    for k in range(8):
        wdeps.append(p.dma("gpsimd", wp_sb[:, k, :], oprojT[k * 128:(k + 1) * 128, :]))
    wdeps.append(p.dma("gpsimd", wg_sb, wg3))
    for dst, src_ in ((bp_sb, oprojb), (g1_sb, ln1g), (b1_sb, ln1b)):
        wdeps.append(p.dma("sync", dst, src_))
    p.wait("tensor", [wdeps[-4], wdeps[-1]])
    p.wait("scalar", wdeps[-1])
    p.wait("vector", wdeps[-1])
    zd = p.run("vector", lambda e: e.memset(zeros8, 0.0), mark=True)
    p.run("vector", lambda e: e.memset(fpad, 0.0))

    r = lambda ap: ap.rearrange("(m p) t -> p m t", p=128)
    lnctx = {}
    for ch in range(4):
        c0 = ch * 512
        p.dma("sync", xnc, r(XN3i)[:, :, c0:c0 + 512])
        ld = p.dma("sync", oac, r(OAf)[:, :, c0:c0 + 512])
        p.wait("tensor", ld)
        p.wait("vector", ld)
        odeps = []
        def evacP(m, chx, ps, dep):
            p.wait("scalar", dep)
            sdep = p.run("scalar", lambda e, _m=m: e.activation(
                y1c[:, _m, :], ps, AF.Identity, bias=bp_sb[:, _m:_m + 1]), mark=True)
            p.wait("vector", sdep)
            p.run("vector", lambda e, _m=m: e.tensor_add(y1c[:, _m, :], y1c[:, _m, :], xnc[:, _m, :]))
            dd = p.run("vector", lambda e, _m=m: e.tensor_copy(y1bc[:, _m, :], y1c[:, _m, :]), mark=True)
            odeps.append(dd)
            return dd
        emit_linear(p, nk=8, nm=8, chunks=[(0, 512)],
                    wf=lambda k, m: wp_sb[:, k, m * 128:(m + 1) * 128],
                    xf=lambda k, _c: oac[:, k, :], evac=evacP)

        def out_cb(k, bnm, brs, _c0=c0):
            t1 = t1s[k % 2]
            p.run("vector", lambda e, _k=k: e.tensor_add(t1, y1c[:, _k, :], bnm))
            p.run("vector", lambda e, _t=t1: e.tensor_mul(t1, t1, brs))
            vdep = p.run("vector", lambda e, _k=k, _t=t1: e.tensor_scalar(
                y1c[:, _k, :], _t, g1_sb[:, _k:_k + 1], b1_sb[:, _k:_k + 1], ALU.mult, ALU.add), mark=True)
            vdep2 = p.run("vector", lambda e, _k=k, _c=_c0: e.tensor_copy(y1bb[:, _k, _c:_c + 512], y1c[:, _k, :]), mark=True)
            return vdep2
        lndep = emit_ln(p, lnctx, xbf=lambda k: y1bc[:, k, :], nk=8, ncol=512,
                        ones_bf=ones_bf, ones_f=ones_f, sq2=sq2, stat=stat, out_cb=out_cb,
                        xdep=odeps[-1])
        p.wait("sync", lndep)
        p.dma("sync", r(Y1)[:, :, c0:c0 + 512], y1c, "os")
        p.dma("sync", r(Y1B)[:, :, c0:c0 + 512], y1bb[:, :, c0:c0 + 512], "os")
        chunk_barrier(p)

    tdeps = []
    for tt in range(16):
        i, ps = p.psum_acquire()
        tiles = [(y1bb[:, k, tt * 128:(tt + 1) * 128], wg_sb[:, k, :], 128, 512) for k in range(8)]
        mdep = emit_matmul_group(p, ps, tiles)
        p.wait("scalar", mdep)
        s1 = p.run("scalar", lambda e, _ps=ps: e.copy(g_f, _ps[:, 0:8]), mark=True)
        p.wait("vector", s1)
        v1 = p.run("vector", lambda e: e.reduce_max(scol[:, 0:1], g_f, axis=AX.X, negate=True), mark=True)
        p.wait("scalar", v1)
        s2 = p.run("scalar", lambda e: e.activation(ex_t, g_f, AF.Exp, bias=scol[:, 0:1],
                                                    accum_out=scol[:, 1:2]), mark=True)
        p.psum_release(i, s1)
        p.wait("vector", s2)
        v2 = p.run("vector", lambda e: e.reciprocal(scol[:, 2:3], scol[:, 1:2]), mark=True)
        p.wait("scalar", v2)
        s3 = p.run("scalar", lambda e: e.activation(prb, ex_t, AF.Identity, scale=scol[:, 2:3]), mark=True)
        p.wait("vector", s3)
        v3 = p.run("vector", lambda e: e.reduce_max(scol[:, 3:4], prb, axis=AX.X), mark=True)
        p.wait("gpsimd", v3)
        g1 = p.run("gpsimd", lambda e: e.tensor_scalar(m1c, prb, scol[:, 3:4], None, ALU.is_equal), mark=True)
        p.wait("vector", g1)
        v4 = p.run("vector", lambda e: e.tensor_tensor_scan(csc, m1c, zeros8, 0.0, ALU.add, ALU.add), mark=True)
        p.wait("gpsimd", v4)
        g2 = p.run("gpsimd", lambda e: e.tensor_scalar(e1c, csc, 1.0, None, ALU.is_equal), mark=True)
        p.wait("vector", g2)
        v5 = p.run("vector", lambda e: e.tensor_mul(e1c, m1c, e1c), mark=True)
        p.wait("gpsimd", [v5])
        g3 = p.run("gpsimd", lambda e, _tt=tt: e.tensor_scalar(
            mask_tm[:, _tt, :], e1c, scol[:, 3:4], None, ALU.mult), mark=True)
        p.wait("vector", g3)
        v6 = p.run("vector", lambda e: e.tensor_copy(fpad[:, 0:8], e1c), mark=True)
        p.wait("tensor", v6)
        td = p.run("tensor", lambda e, _tt=tt: e.transpose(psT[0:16, 0:128], fpad, ident), mark=True)
        p.wait("vector", td)
        tdep = p.run("vector", lambda e, _tt=tt: e.tensor_copy(ohfm[:, _tt * 128:(_tt + 1) * 128], psT[0:16, 0:128]), mark=True)
        tdeps.append(tdep)
        p.wait("scalar", tdep)
        p.wait("tensor", tdep)
    vS = p.sem("vs")
    p.wait("sync", (vS, vS.n))
    p.dma("sync", MASKA, mask_tm.rearrange("p t e -> p (t e)"), "os")
    p.wait("vector", zd)
    c1 = p.run("vector", lambda e: e.tensor_tensor_scan(
        cumf[0:8, :], ohfm[0:8, :], zeros16[0:8, :], 0.0, ALU.add, ALU.add), mark=True)
    p.wait("gpsimd", c1)
    g5 = p.run("gpsimd", lambda e: e.tensor_mul(tmp8, ohfm[0:8, :], cumf[0:8, :]), mark=True)
    p.wait("tensor", c1)
    icr, pscr = p.psum_acquire()
    crd = emit_matmul_group(p, pscr, [(cumf[0:8, TC - 1:TC], eye8pad, 1, 128)])
    p.wait("vector", crd)
    crv = p.run("vector", lambda e, _ps=pscr: e.tensor_copy(locsb[:, 0:128], _ps[0:1, 0:128]), mark=True)
    p.psum_release(icr, crv)
    p.wait("sync", crv)
    p.dma("sync", CNTR[0:1, :], locsb[:, 0:128], "os")
    p.wait("vector", crv)
    p.wait("tensor", g5)
    ldep = None
    for chx in range(4):
        il, psl = p.psum_acquire()
        dl = emit_matmul_group(p, psl, [(ones8f, tmp8[:, chx * 512:(chx + 1) * 512], 1, 512)])
        p.wait("vector", dl)
        ldep = p.run("vector", lambda e, _c=chx, _ps=psl: e.tensor_copy(
            locsb[:, _c * 512:(_c + 1) * 512], _ps[0:1, :]), mark=True)
        p.psum_release(il, ldep)
    p.wait("sync", ldep)
    p.dma("sync", LOCROW, locsb, "os")
    chunk_barrier(p)
    return p


def prep_l3ffnB(ins, o7, oA, c):
    b, hf = c // 2, c % 2
    t0 = hf * TC
    rows = []
    for pc in range(4):
        n0 = t0 + pc * 512
        rows.append(oA[n0 // 512]["OA"][b])
    of = np.concatenate(rows, axis=0)
    return {
        "XN3i": o7[c]["XN3"], "OAf": np.ascontiguousarray(of.T),
        "oprojT": _f32(np.asarray(ins["L3_ffn_out_proj_w"]).T), "oprojb": col_tiles(ins["L3_ffn_out_proj_b"]),
        "ln1g": col_tiles(ins["L3_ffn_ln1_g"]), "ln1b": col_tiles(ins["L3_ffn_ln1_b"]),
        "wg3": _f32(np.tile(np.asarray(ins["L3_ffn_wg"], np.float32)
                            .reshape(8, 128, 8).transpose(1, 0, 2), (1, 1, 64))),
    }


def build_l3ffnD(p=None, io=None):
    p = p or Prog("l3ffnD")
    io = io or ExtIO(p)
    X3i = io.inp("X3i", [D, TC])
    XN3i = io.inp("XN3i", [D, TC])
    Y3re = io.inp("Y3re", [D, TC])
    Y1i = io.inp("Y1i", [D, TC])
    Y1Bi = io.inp("Y1Bi", [D, TC], BF16)
    MASKAi = io.inp("MASKAi", [128, 128])
    LOCROWi = io.inp("LOCROWi", [1, TC])
    CNTS8 = io.inp("CNTS8", [8, 128])
    PREVM = io.inp("PREVM", [8, 1])
    WCall = io.inp("WCall", [8, D, D])
    CEall = io.inp("CEall", [8, 1, D])
    dcol3 = io.inp("dcol3", [128, 8])
    ln2g = io.inp("ln2g", [128, 8])
    ln2b = io.inp("ln2b", [128, 8])
    X4 = io.out("X4", [D, TC])

    p.regconsts(EPS, 1.0, -1.0, -0.5, 2048.5)
    ones_bf, ob_dep = p.const_sbuf(np.ones((128, 1), dtype=BF))
    ones_f, of_dep = p.const_sbuf(np.ones((1, 128), dtype=np.float32))
    ones8f, o8_dep = p.const_sbuf(np.ones((8, 1), dtype=np.float32))
    ident, id_dep = p.const_sbuf(np.eye(128, dtype=BF))
    p.wait("tensor", [ob_dep, of_dep, o8_dep, id_dep])
    p.psum_pool(7)
    psT = p.psum_t("psT", [128, 1024], BF16)
    r = lambda ap: ap.rearrange("(m p) t -> p m t", p=128)

    maska = p.sbuf("maska", [128, 16, 8], F32)
    cnts = p.sbuf("cnts", [8, 128], F32)
    prevm = p.sbuf("prevm", [8, 1], F32)
    offrow = p.sbuf("offrow", [1, 128], F32)
    offb = p.sbuf("offb", [128, 128], F32)
    loc_tm = p.sbuf("loc_tm", [128, 16], F32)
    offs_tm = p.sbuf("offs_tm", [128, 16], F32)
    keep_tm = p.sbuf("keep_tm", [128, 16], F32)
    oh_tm = p.sbuf("oh_tm", [128, 8], F32)
    t8 = p.sbuf("t8", [128, 8], F32)
    scale_tm = p.sbuf("scale_tm", [128, 16, 8], F32)
    d1 = p.dma("sync", maska, MASKAi.rearrange("p (t e) -> p t e", e=8))
    d2 = p.dma("sync", cnts, CNTS8)
    d2 = p.dma("sync", prevm, PREVM)
    for j in range(16):
        d3 = p.dma("sync", loc_tm[:, j:j + 1], LOCROWi[:, j * 128:(j + 1) * 128].rearrange("one p -> p one"))
    p.wait("tensor", d2)
    io_, pso = p.psum_acquire()
    od = emit_matmul_group(p, pso, [(prevm, cnts, 1, 128)])
    p.wait("vector", od)
    ov = p.run("vector", lambda e, _ps=pso: e.tensor_copy(offrow, _ps[0:1, 0:128]), mark=True)
    p.psum_release(io_, ov)
    p.wait("tensor", ov)
    ib, psb = p.psum_acquire()
    bd = emit_matmul_group(p, psb, [(ones_f, offrow, 128, 128)])
    p.wait("vector", [bd, d1, d3])
    obv = p.run("vector", lambda e, _ps=psb: e.tensor_copy(offb, _ps[:, 0:128]), mark=True)
    p.psum_release(ib, obv)
    vdep = None
    for tt in range(16):
        p.run("vector", lambda e, _t=tt: e.tensor_scalar(oh_tm, maska[:, _t, :], 0.0, None, ALU.is_gt))
        p.run("vector", lambda e: e.tensor_mul(t8, oh_tm, offb[:, 0:8]))
        p.run("vector", lambda e, _t=tt: e.reduce_sum(offs_tm[:, _t:_t + 1], t8, axis=AX.X))
        vdep = p.run("vector", lambda e: e.engine_nop(), mark=True)
    p.run("vector", lambda e: e.tensor_add(keep_tm, loc_tm, offs_tm))
    kd = p.run("vector", lambda e: e.tensor_scalar(keep_tm, keep_tm, 2048.5, None, ALU.is_lt), mark=True)
    sdep = None
    for tt in range(16):
        sdep = p.run("vector", lambda e, _t=tt: e.tensor_tensor(
            scale_tm[:, _t, :], maska[:, _t, :],
            keep_tm[:, _t:_t + 1].broadcast_to([128, 8]), ALU.mult), mark=True)
    p.barrier()

    moefm = p.sbuf("moefm", [128, 8, TC], BF16)
    stZ = ExitStack()
    sbZ = lambda nm, shp, dt_: stZ.enter_context(p.nc.sbuf_tensor(nm, list(shp), dt_))[:]
    y1b = sbZ("y1b", [128, 8, TC], BF16)
    acc = sbZ("acc", [128, 16, 1024], F32)
    wc = [sbZ("wc0", [128, 8, 1024], BF16), sbZ("wc1", [128, 8, 1024], BF16)]
    ce_sb = sbZ("ce_sb", [1, 1024], F32)
    cb_sb = sbZ("cb_sb", [128, 512], F32)
    yd = p.dma("sync", y1b, r(Y1Bi))
    p.wait("tensor", yd)
    for e_i in range(8):
        pr = e_i % 2
        tS = p.sem("ts")
        p.wait("gpsimd", (tS, tS.n))
        wd = None
        for k in range(8):
            wd = p.dma("gpsimd", wc[pr][:, k, :], WCall[e_i, k * 128:(k + 1) * 128, :])
        ced = p.dma("sync", ce_sb, CEall[e_i])
        p.wait("tensor", [wd, ced])
        for dn in range(2):
            ic, psc = p.psum_acquire()
            cd = emit_matmul_group(p, psc, [(ones_f, ce_sb[:, dn * 512:(dn + 1) * 512], 128, 512)])
            p.wait("vector", cd)
            cbv = p.run("vector", lambda e, _ps=psc: e.tensor_copy(cb_sb, _ps), mark=True)
            p.psum_release(ic, cbv)
            for tt in range(16):
                iz, psz = p.psum_acquire()
                zd = emit_matmul_group(p, psz, [
                    (y1b[:, k, tt * 128:(tt + 1) * 128], wc[pr][:, k, dn * 512:(dn + 1) * 512], 128, 512)
                    for k in range(8)])
                p.wait("vector", [zd, cbv])
                if e_i == 0:
                    p.run("vector", lambda e, _t=tt, _d=dn, _ps=psz: e.tensor_scalar(
                        acc[:, _t, _d * 512:(_d + 1) * 512], _ps, scale_tm[:, _t, 0:1], None, ALU.mult))
                else:
                    p.run("vector", lambda e, _t=tt, _d=dn, _ps=psz, _e=e_i: e.scalar_tensor_tensor(
                        acc[:, _t, _d * 512:(_d + 1) * 512], _ps, scale_tm[:, _t, _e:_e + 1],
                        acc[:, _t, _d * 512:(_d + 1) * 512], ALU.mult, ALU.add))
                azd = p.run("vector", lambda e, _t=tt, _d=dn, _e=e_i: e.scalar_tensor_tensor(
                    acc[:, _t, _d * 512:(_d + 1) * 512], cb_sb, scale_tm[:, _t, _e:_e + 1],
                    acc[:, _t, _d * 512:(_d + 1) * 512], ALU.mult, ALU.add), mark=True)
                p.psum_release(iz, azd)
    p.barrier()

    accb = sbZ("accb", [128, 128], BF16)
    for tt in range(16):
        for dm in range(8):
            cdep = p.run("vector", lambda e, _t=tt, _d=dm: e.tensor_copy(
                accb, acc[:, _t, _d * 128:(_d + 1) * 128]), mark=True)
            p.wait("tensor", cdep)
            td = p.run("tensor", lambda e: e.transpose(psT[:, 0:128], accb, ident), mark=True)
            p.wait("vector", td)
            vd_ = p.run("vector", lambda e, _t=tt, _d=dm: e.tensor_copy(
                moefm[:, _d, _t * 128:(_t + 1) * 128], psT[:, 0:128]), mark=True)
            p.wait("tensor", vd_)
    p.barrier()
    stZ.close()

    stF = ExitStack()
    sbF = lambda nm, shp, dt_: stF.enter_context(p.nc.sbuf_tensor(nm, list(shp), dt_))[:]
    y1c = sbF("y1c", [128, 8, 512], F32)
    sumc = sbF("sumc", [128, 8, 512], F32)
    sumb = sbF("sumb", [128, 8, 512], BF16)
    x3c = sbF("x3c", [128, 8, 512], F32)
    xnc = sbF("xnc", [128, 8, 512], F32)
    y3c = sbF("y3c", [128, 8, 512], F32)
    g2_sb = sbF("g2_sb", [128, 8], F32)
    b2_sb = sbF("b2_sb", [128, 8], F32)
    dc_sb = sbF("dc_sb", [128, 8], F32)
    sq2 = [sbF("sq2a", [128, 512], BF16), sbF("sq2b", [128, 512], BF16)]
    stat = sbF("stat", [1, 2048], F32)
    t1s = [sbF("t1a", [128, 512], F32), sbF("t1b", [128, 512], F32)]
    for dst, src_ in ((g2_sb, ln2g), (b2_sb, ln2b), (dc_sb, dcol3)):
        bdp = p.dma("sync", dst, src_)
    p.wait("vector", bdp)
    p.wait("scalar", bdp)
    lnctx = {}
    for ch in range(4):
        c0 = ch * 512
        p.dma("sync", y1c, r(Y1i)[:, :, c0:c0 + 512])
        p.dma("sync", x3c, r(X3i)[:, :, c0:c0 + 512])
        p.dma("sync", xnc, r(XN3i)[:, :, c0:c0 + 512])
        ld = p.dma("sync", y3c, r(Y3re)[:, :, c0:c0 + 512])
        p.wait("vector", ld)
        vdep = None
        for m in range(8):
            p.run("vector", lambda e, _m=m, _c=c0: e.tensor_add(
                sumc[:, _m, :], y1c[:, _m, :], moefm[:, _m, _c:_c + 512]))
            vdep = p.run("vector", lambda e, _m=m: e.tensor_copy(sumb[:, _m, :], sumc[:, _m, :]), mark=True)

        def out_cb(k, bnm, brs):
            t1 = t1s[k % 2]
            p.run("vector", lambda e, _k=k: e.tensor_add(t1, sumc[:, _k, :], bnm))
            p.run("vector", lambda e, _t=t1: e.tensor_mul(t1, t1, brs))
            p.run("vector", lambda e, _k=k, _t=t1: e.tensor_scalar(
                t1, _t, g2_sb[:, _k:_k + 1], b2_sb[:, _k:_k + 1], ALU.mult, ALU.add))
            p.run("vector", lambda e, _k=k: e.scalar_tensor_tensor(
                x3c[:, _k, :], xnc[:, _k, :], dc_sb[:, _k:_k + 1], x3c[:, _k, :], ALU.mult, ALU.add))
            p.run("vector", lambda e, _k=k: e.tensor_add(x3c[:, _k, :], x3c[:, _k, :], y3c[:, _k, :]))
            vdep2 = p.run("vector", lambda e, _k=k, _t=t1: e.tensor_add(x3c[:, _k, :], x3c[:, _k, :], _t), mark=True)
            return vdep2
        lndep = emit_ln(p, lnctx, xbf=lambda k: sumb[:, k, :], nk=8, ncol=512,
                        ones_bf=ones_bf, ones_f=ones_f, sq2=sq2, stat=stat, out_cb=out_cb,
                        xdep=vdep)
        p.wait("sync", lndep)
        p.dma("sync", r(X4)[:, :, c0:c0 + 512], x3c, "os")
        chunk_barrier(p)
    stF.close()
    return p


def prep_l3ffnD(ins, o1, o7, o8, oB, c):
    b, hf = c // 2, c % 2
    t0 = hf * TC
    y3 = np.concatenate([o8[2 * b]["Y"][:, t0:t0 + TC], o8[2 * b + 1]["Y"][:, t0:t0 + TC]], axis=0)
    cnts8 = np.stack([oB[cc]["CNTR"][0] for cc in range(8)])
    prevm = np.zeros((8, 1), np.float32)
    prevm[:c] = 1.0
    wcall = np.stack([o1[e]["WCOMB"] for e in range(8)])
    ceall = np.stack([o1[e]["CE"] for e in range(8)])
    return {
        "X3i": o7[c]["X3"], "XN3i": o7[c]["XN3"], "Y3re": _f32(y3),
        "Y1i": oB[c]["Y1"], "Y1Bi": oB[c]["Y1B"], "MASKAi": oB[c]["MASKA"],
        "LOCROWi": oB[c]["LOCROW"], "CNTS8": _f32(cnts8), "PREVM": prevm,
        "WCall": wcall, "CEall": ceall,
        "dcol3": col_tiles(ins["L3_attn_D"]),
        "ln2g": col_tiles(ins["L3_ffn_ln2_g"]), "ln2b": col_tiles(ins["L3_ffn_ln2_b"]),
    }


_BUILD_CACHE = {}


def _get(name, fn):
    if name not in _BUILD_CACHE:
        p = fn()
        _BUILD_CACHE[name] = p.build()
    return _BUILD_CACHE[name]


def kernel(**ins):
    cores = list(range(NCORES))

    nc1 = _get("l0pre", build_l0pre)
    o1 = run_bass_kernel_spmd(nc1, [prep_l0pre(ins, c) for c in cores], core_ids=cores).results
    nc2 = _get("l0scan", build_l0scan)
    o2 = run_bass_kernel_spmd(nc2, [prep_l0scan(ins, o1, c) for c in cores], core_ids=cores).results
    nc3 = _get("l0post", build_l0post)
    o3 = run_bass_kernel_spmd(nc3, [prep_l0post(ins, o1, o2, c) for c in cores], core_ids=cores).results
    nc3b = _get("l1ffn", build_l1ffn)
    o3b = run_bass_kernel_spmd(nc3b, [prep_l1ffn(ins, o3, c) for c in cores], core_ids=cores).results
    nc4 = _get("l1attn", build_l1attn)
    o4 = run_bass_kernel_spmd(nc4, [prep_l1attn(ins, o3, c) for c in cores], core_ids=cores).results
    nc5 = _get("l2pre", build_l2pre)
    o5 = run_bass_kernel_spmd(nc5, [prep_l2pre(ins, o3, o4, o3b, c) for c in cores], core_ids=cores).results
    nc6 = _get("l2attn", build_l2attn)
    o6 = run_bass_kernel_spmd(nc6, [prep_l2attn(ins, o5, c) for c in cores], core_ids=cores).results
    nc7 = _get("l3pre", build_l3pre)
    o7 = run_bass_kernel_spmd(nc7, [prep_l3pre(ins, o5, o6, c) for c in cores], core_ids=cores).results
    nc8 = _get("l3scan", build_l3scan)
    o8 = run_bass_kernel_spmd(nc8, [prep_l3scan(ins, o7, c) for c in cores], core_ids=cores).results
    nc9 = _get("l3ffnA", build_l3ffnA)
    o9 = run_bass_kernel_spmd(nc9, [prep_l3ffnA(ins, o7, c) for c in cores], core_ids=cores).results
    nc10 = _get("l3ffnB", build_l3ffnB)
    o10 = run_bass_kernel_spmd(nc10, [prep_l3ffnB(ins, o7, o9, c) for c in cores], core_ids=cores).results
    nc11 = _get("l3ffnD", build_l3ffnD)
    o11 = run_bass_kernel_spmd(nc11, [prep_l3ffnD(ins, o1, o7, o8, o10, c) for c in cores], core_ids=cores).results

    out = np.empty((B, S, D), dtype=np.float32)
    for c in cores:
        b, hf = c // 2, c % 2
        out[b, hf * TC:(hf + 1) * TC, :] = np.asarray(o11[c]["X4"], np.float32).T
    return out
